# version 27
# baseline (speedup 1.0000x reference)
import numpy as np
import concourse.bass as bass
import concourse.mybir as mybir
from concourse.bass_utils import run_bass_kernel_spmd
from concourse.alu_op_type import AluOpType as OP
from contextlib import ExitStack

F32 = mybir.dt.float32
BF16 = mybir.dt.bfloat16
AF = mybir.ActivationFunctionType

SQ = 16384
CM = 1024
NCC = 8
H = 16
HD = 64
TOK = 2048
HTOK = 1024
KVA = 1152
XCOLS = 2176
NKT = 9
MASKVAL = -1e30

BLK_S = [(0, 512), (512, 512), (1024, 128)]
BLK_Q = [(32, 512), (544, 257), (801, 256)]
BLK_O = [(33, 512), (545, 512)]
QEND = 1057

WNAMES = ["wq", "wk", "wv", "wo", "wr", "wk2", "wv2"]

_NC_CACHE = {}
_BUILD_LOG = {}
PH = [""]
STOP_PHASE = 99
DISABLE = set()
DBG = None
DBG_H = 0
DBG_G = 0


class _StopBuild(Exception):
    pass


_CURH = [0]


def _ckpt(n):
    if STOP_PHASE <= n + 10 * _CURH[0]:
        raise _StopBuild()


def build_program():
    nc = bass.Bass("TRN2", target_bir_lowering=False)

    xTs = nc.dram_tensor("xTs", [CM, XCOLS], F32, kind="ExternalInput").ap()
    wd = {n: nc.dram_tensor(n, [CM, CM], F32, kind="ExternalInput").ap() for n in WNAMES}
    qb8_d = nc.dram_tensor("qb8", [128, 8], F32, kind="ExternalInput").ap()
    ob8_d = nc.dram_tensor("ob8", [128, 8], F32, kind="ExternalInput").ap()
    mur8_d = nc.dram_tensor("mur8", [128, 8], F32, kind="ExternalInput").ap()
    muk8_d = nc.dram_tensor("muk8", [128, 8], F32, kind="ExternalInput").ap()
    kbrow_d = nc.dram_tensor("kbrow", [1, CM], F32, kind="ExternalInput").ap()
    vbrow_d = nc.dram_tensor("vbrow", [1, CM], F32, kind="ExternalInput").ap()
    kvm_d = nc.dram_tensor("kvm", [1, XCOLS], F32, kind="ExternalInput").ap()
    mask_d = nc.dram_tensor("mask4", [128, 768], F32, kind="ExternalInput").ap()
    idn_d = nc.dram_tensor("idn", [128, 128], F32, kind="ExternalInput").ap()
    yT = nc.dram_tensor("yT", [CM, TOK], F32, kind="ExternalOutput").ap()
    dbg = nc.dram_tensor("dbg", [128, NKT * 1024], BF16, kind="ExternalOutput").ap()

    OPS = []
    CNT = {k: 0 for k in ("T", "V", "S", "GD", "SD")}
    TOKENG = {}

    def op(eng, builder, waits=(), inc=None, amt=None):
        tok = None
        if inc is not None:
            if amt is None:
                amt = 16 if inc in ("GD", "SD") else 1
            CNT[inc] += amt
            tok = (inc, CNT[inc])
            TOKENG[tok] = eng
        ws = []
        for t in waits:
            if t is None:
                continue
            if TOKENG.get(t) == eng and t[0] not in ("GD", "SD"):
                continue
            ws.append(t)
        OPS.append((eng, builder, ws, inc, amt, PH[0]))
        return tok

    def mark(eng, inc, amt=None):
        if amt is None:
            amt = 16 if inc in ("GD", "SD") else 1
        for i in range(len(OPS) - 1, -1, -1):
            if OPS[i][0] == eng:
                e, b, ws, oldinc, oldamt, phl = OPS[i]
                assert oldinc is None, "op already carries an inc"
                CNT[inc] += amt
                tok = (inc, CNT[inc])
                TOKENG[tok] = eng
                OPS[i] = (e, b, ws, inc, amt, phl)
                return tok
        raise RuntimeError("no op to mark")

    with ExitStack() as stk:
        en = stk.enter_context
        arnA = [en(nc.sbuf_tensor(f"arnA{i}", [128, KVA], BF16)) for i in range(NCC)]
        arnS = [en(nc.sbuf_tensor(f"arnS{i}", [128, KVA], BF16)) for i in range(NCC)]
        arnQ = [en(nc.sbuf_tensor(f"arnQ{i}", [128, KVA], BF16)) for i in range(NCC)]
        arnK = [en(nc.sbuf_tensor(f"arnK{i}", [128, KVA], BF16)) for i in range(NCC)]
        arnV = en(nc.sbuf_tensor("arnV", [128, NKT * 1024], BF16))
        x2 = [en(nc.sbuf_tensor(f"x2_{i}", [128, KVA], F32)) for i in range(NCC)]
        ptb = en(nc.sbuf_tensor("ptb", [128, 2 * 768], BF16))
        wbuf = [en(nc.sbuf_tensor(f"wbuf{i}", [128, NCC * 1024], BF16)) for i in range(2)]
        rzsb = [en(nc.sbuf_tensor(f"rzsb{i}", [128, 512], BF16)) for i in range(2)]
        rztmp = en(nc.sbuf_tensor("rztmp", [128, 512], F32))
        xstage = en(nc.sbuf_tensor("xstage", [128, 1024], F32))
        bstage = en(nc.sbuf_tensor("bstage", [128, 512], BF16))
        atmp = en(nc.sbuf_tensor("atmp", [128, 1024], BF16))
        rT1 = en(nc.sbuf_tensor("rT1", [1, KVA], F32))
        rSQ = en(nc.sbuf_tensor("rSQ", [1, KVA], F32))
        rA = {k: en(nc.sbuf_tensor(f"rA_{k}", [1, KVA], BF16)) for k in ("1", "a", "f")}
        rA["2"] = rA["1"]
        rMS = {k: en(nc.sbuf_tensor(f"rMS_{k}", [1, KVA], BF16)) for k in ("1", "a")}
        rMS["2"] = rMS["1"]
        rD = en(nc.sbuf_tensor("rD", [1, KVA], BF16))
        onesb = en(nc.sbuf_tensor("onesb", [128, 1], BF16))
        epsb = en(nc.sbuf_tensor("epsb", [1, 1], F32))
        epsz = en(nc.sbuf_tensor("epsz", [128, 1], F32))
        sel = {r: en(nc.sbuf_tensor(f"sel{r}", [128, 64], BF16)) for r in (0, 32, 64)}
        ones1 = en(nc.sbuf_tensor("ones1", [1, 128], BF16))
        onesdiv = en(nc.sbuf_tensor("onesdiv", [1, 128], BF16))
        qb_s = en(nc.sbuf_tensor("qb_s", [128, 8], F32))
        ob_s = en(nc.sbuf_tensor("ob_s", [128, 8], F32))
        mur_s = en(nc.sbuf_tensor("mur_s", [128, 8], F32))
        muk_s = en(nc.sbuf_tensor("muk_s", [128, 8], F32))
        kbrow = en(nc.sbuf_tensor("kbrow_s", [1, CM], BF16))
        vbrow = en(nc.sbuf_tensor("vbrow_s", [1, CM], BF16))
        kvm = en(nc.sbuf_tensor("kvm_s", [1, XCOLS], BF16))
        mask4 = en(nc.sbuf_tensor("mask4_s", [128, 768], BF16))
        idn = en(nc.sbuf_tensor("idn_s", [128, 128], BF16))
        psA = en(nc.psum_tensor("psA", [128, 1536], F32))
        psB = en(nc.psum_tensor("psB", [128, 1536], F32))
        psC = en(nc.psum_tensor("psC", [128, 1024], F32))
        sems = {k: en(nc.semaphore(f"sem{k}")) for k in ("T", "V", "S", "GD", "SD")}
        en(nc.allow_low_precision(
            "bf16 row normalizers / bf16 matmul operands by design; all "
            "accumulation stays fp32 in PSUM"))
        block = en(nc.Block())

        for dst, src in ((qb_s, qb8_d), (ob_s, ob8_d), (mur_s, mur8_d), (muk_s, muk8_d)):
            op("sdma", lambda s, d=dst, so=src: s.dma_start(out=d[:], in_=so), inc="SD")
        gl = []
        for dst, src in ((kbrow, kbrow_d), (vbrow, vbrow_d), (kvm, kvm_d),
                         (mask4, mask_d), (idn, idn_d)):
            gl.append(op("gdma", lambda g, d=dst, so=src: g.dma_start(out=d[:], in_=so),
                         inc="GD"))
        t_consts_gd = gl[-1]
        op("vec", lambda v: v.memset(onesb[:], 1.0))
        op("vec", lambda v: v.memset(epsb[:], 1e-5))
        op("vec", lambda v: v.memset(epsz[:], 1e-30))
        op("vec", lambda v: v.memset(ones1[:], 1.0))
        for r in (0, 32, 64):
            op("vec", lambda v, rr=r: v.memset(sel[rr][:, 0:64], 0.0))
            op("vec", lambda v, rr=r: v.memset(sel[rr][rr:rr + 1, 0:64], 1.0))
        op("vec", lambda v: v.memset(atmp[:, 0:512], 1.0))
        t_ones = op("vec", lambda v: v.memset(onesdiv[:], 1.0 / 1024.0), inc="V")
        op("tensor", lambda t: t.matmul(psA[:, 1024:1536], idn[:], atmp[:, 0:512],
                                        start=True, stop=True),
           waits=(t_ones, t_consts_gd))
        op("tensor", lambda t: t.matmul(psB[:, 1024:1536], idn[:], atmp[:, 0:512],
                                        start=True, stop=True))

        def load_weight(name, bufi, waits):
            last = None
            for cc in range(NCC):
                last = op("gdma",
                          lambda g, b=wbuf[bufi], c=cc, n=name: g.dma_start(
                              out=b[:, 1024 * c:1024 * c + 1024],
                              in_=wd[n][128 * c:128 * c + 128, :]),
                          waits=waits if cc == 0 else (), inc="GD")
            return last

        def ln_stats(src_fn, blocks, shift, key, waits, sq_fn, want_ms=True,
                     src_toks=None):
            first = True
            last = None
            for bi, (c0, n) in enumerate(blocks):
                o0 = c0 - shift
                for cc in range(NCC):
                    ws = waits if first else ()
                    if src_toks is not None:
                        ws = tuple(ws) + tuple(src_toks.get((cc, bi), ()))
                    last = op("tensor",
                              lambda t, c=cc, a=c0, m=n, o=o0, st=(cc == 0): t.matmul(
                                  psA[0:1, o:o + m], onesb[:], src_fn(c, a, m),
                                  start=st, stop=True),
                              waits=ws)
                    first = False
            t_smm = mark("tensor", "T")
            last = None
            for cc in range(NCC):
                for bi, (c0, n) in enumerate(blocks):
                    ws = waits
                    if src_toks is not None:
                        ws = tuple(ws) + tuple(src_toks.get((cc, bi), ()))
                    last = op("vec", lambda v, c=cc, a=c0, m=n: v.tensor_tensor(
                        sq_fn(c, a, m), src_fn(c, a, m), src_fn(c, a, m), OP.mult),
                        waits=ws)
            t_sq = op("vec", lambda v: v.memset(rT1[0:1, KVA - 1:KVA], 0.0), inc="V")
            first = True
            for (c0, n) in blocks:
                o0 = c0 - shift
                for cc in range(NCC):
                    op("tensor", lambda t, c=cc, a=c0, m=n, o=o0, st=(cc == 0): t.matmul(
                        psB[0:1, o:o + m], onesb[:], sq_fn(c, a, m), start=st, stop=True),
                        waits=(t_sq,) if first else ())
                    first = False
            t_qmm = mark("tensor", "T")
            W = blocks[-1][0] + blocks[-1][1] - shift
            t_s2 = op("scalar", lambda s: s.activation(
                rT1[0:1, 0:W], psA[0:1, 0:W], AF.Square, bias=0.0, scale=1.0),
                waits=(t_smm,), inc="S")
            t_u = op("vec", lambda v: v.scalar_tensor_tensor(
                rT1[0:1, 0:W], psB[0:1, 0:W], 1024.0, rT1[0:1, 0:W],
                OP.mult, OP.subtract), waits=(t_s2, t_qmm), inc="V")
            op("scalar", lambda s: s.activation(
                rT1[0:1, 0:W], rT1[0:1, 0:W], AF.Ln, bias=epsb[0:1, 0:1],
                scale=2.0 ** -20), waits=(t_u,))
            t_a = op("scalar", lambda s: s.activation(
                rA[key][0:1, 0:W], rT1[0:1, 0:W], AF.Exp, bias=0.0,
                scale=-0.5), inc="S")
            if want_ms:
                return op("vec", lambda v: v.tensor_tensor(
                    rMS[key][0:1, 0:W], psA[0:1, 0:W], rA[key][0:1, 0:W], OP.mult),
                    waits=(t_a,), inc="V")
            return op("scalar", lambda s: s.activation(
                rSQ[0:1, 0:W], rT1[0:1, 0:W], AF.Exp, bias=0.0,
                scale=0.5), inc="S")

        def ln_apply2(src_fn, mid, dst, blocks, shift, key, waits):
            prevtok = [None, None]
            last = None
            tokmap = {}
            for bi, (c0, n) in enumerate(blocks):
                o0 = c0 - shift
                pb = psC if bi % 2 == 0 else psA
                w0 = list(waits) + ([prevtok[bi % 2]] if prevtok[bi % 2] else [])
                op("tensor", lambda t, p=pb, o=o0, m=n: t.matmul(
                    p[:, 0:m], onesdiv[:], rMS[key][0:1, o:o + m], start=True, stop=True),
                    waits=tuple(w0))
                tb = op("tensor", lambda t, p=pb, o=o0, m=n: t.matmul(
                    p[:, 512:512 + m], ones1[:], rA[key][0:1, o:o + m],
                    start=True, stop=True), inc="T")
                for cc in range(NCC):
                    op("vec", lambda v, c=cc, a=c0, m=n, p=pb: v.tensor_tensor(
                        mid[c][:, a:a + m], src_fn(c, a, m), p[:, 0:m], OP.subtract),
                        waits=(tb,) if cc == 0 else ())
                for cc in range(NCC):
                    tokmap[(bi, cc)] = op(
                        "vec", lambda v, c=cc, a=c0, m=n, p=pb: v.tensor_tensor(
                            dst[c][:, a:a + m], mid[c][:, a:a + m], p[:, 512:512 + m],
                            OP.mult), waits=(), inc="V")
                last = op("vec", lambda v: v.memset(rT1[0:1, KVA - 1:KVA], 0.0), inc="V")
                prevtok[bi % 2] = last
            return last, tokmap

        def proj_cm(bufi, w_tok, src_fn, blocks, evac, waits, bias_fold=None, XL=0,
                    src_toks=None):
            evac_tok = [None, None]
            t_mm_last = None
            last_ev = None
            for oc in range(NCC):
                pp = psA if oc % 2 == 0 else psB
                w0 = [w_tok] + list(waits) + ([evac_tok[oc % 2]] if evac_tok[oc % 2] else [])
                first = True
                for bi, (c0, n) in enumerate(blocks):
                    for cc in range(NCC):
                        ws = tuple(w0) if first else ()
                        if src_toks is not None:
                            ws = ws + tuple(src_toks.get((cc, bi), ()))
                        op("tensor", lambda t, p=pp, c=cc, o=oc, a=c0, m=n, b=bi,
                           st=(cc == 0), bf=bufi: t.matmul(
                               p[:, 512 * b:512 * b + m],
                               wbuf[bf][:, 1024 * c + 128 * o:1024 * c + 128 * o + 128],
                               src_fn(c, a, m), start=st, stop=True),
                           waits=ws)
                        first = False
                if bias_fold is not None:
                    brow = bias_fold
                    for bi, (c0, n) in enumerate(blocks):
                        op("tensor", lambda t, p=pp, o=oc, a=c0, m=n, b=bi: t.matmul(
                            p[:, 512 * b:512 * b + m],
                            brow[0:1, 128 * o:128 * o + 128],
                            kvm[0:1, XL + a:XL + a + m], start=False, stop=True))
                t_mm_last = mark("tensor", "T")
                for bi, (c0, n) in enumerate(blocks):
                    last_ev = evac(oc, bi, c0, n, pp, (t_mm_last,) if bi == 0 else ())
                evac_tok[oc % 2] = last_ev
            return t_mm_last, last_ev

        def proj_tm(w_tok, waits, XL, src_toks=None):
            evac_tok = [None, None]
            t_mm = None
            last_ev = None
            for tt in range(NKT):
                pp = psA if tt % 2 == 0 else psB
                w0 = [w_tok] + list(waits) + ([evac_tok[tt % 2]] if evac_tok[tt % 2] else [])
                first = True
                bt = min(tt // 4, 2)
                for cc in range(NCC):
                    for dv in range(2):
                        ws = tuple(w0) if first else ()
                        if src_toks is not None and dv == 0:
                            ws = ws + tuple(src_toks.get((cc, bt), ()))
                        op("tensor", lambda t, p=pp, c=cc, k=tt, d=dv, st=(cc == 0):
                           t.matmul(p[:, 512 * d:512 * d + 512],
                                    arnA[c][:, 128 * k:128 * k + 128],
                                    wbuf[0][:, 1024 * c + 512 * d:1024 * c + 512 * d + 512],
                                    start=st, stop=True),
                           waits=ws)
                        first = False
                for dv in range(2):
                    op("tensor", lambda t, p=pp, k=tt, d=dv: t.matmul(
                        p[:, 512 * d:512 * d + 512],
                        kvm[0:1, XL + 128 * k:XL + 128 * k + 128],
                        vbrow[0:1, 512 * d:512 * d + 512], start=False, stop=True))
                t_mm = mark("tensor", "T")
                last_ev = op("scalar", lambda s, p=pp, k=tt: s.activation(
                    arnV[:, 1024 * k:1024 * k + 1024], p[:, 0:1024], AF.Copy,
                    bias=0.0, scale=1.0), waits=(t_mm,), inc="S")
                evac_tok[tt % 2] = last_ev
            return t_mm, last_ev

        att_state = {"exp": [None, None], "msk": [None, None], "ki": 0,
                     "evacV": None, "evacT": None}

        def attention(waits):
            st = att_state
            groups = []
            for (Q0, QW) in BLK_Q:
                kts = [kt for kt in range(NKT)
                       if max(128 * kt - 32, 32) < Q0 + QW
                       and min(128 * kt + 160, QEND) > Q0]
                for g in range(4):
                    groups.append((Q0, QW, g, kts))

            def emit_tail(p):
                bc = []
                for (pz, r, zi, rr) in (
                        (psA, 0, 0, 0), (psA, 64, 0, 32),
                        (psB, 0, 0, 64), (psB, 64, 1, 0)):
                    bc.append(op("tensor",
                                 lambda t, pzz=pz, ro=r, z=zi, rw=rr, q=p["QW"]:
                                 t.matmul(
                                     pzz[ro:ro + 64, 1024:1024 + q],
                                     sel[rw][:, 0:64], rzsb[z][:, 0:q],
                                     start=True, stop=True),
                                 waits=(p["t_r1"],) if not bc else ()))
                t_bc = mark("tensor", "T")
                op("vec", lambda v, g_=p["g"], q0=p["Q0"], q=p["QW"]: v.tensor_tensor(
                    arnA[2 * g_][:, q0:q0 + q], atmp[:, 0:q],
                    psA[:, 1024:1024 + q], OP.mult), waits=(t_bc, p["t_cp"]))
                t_ev = op("vec", lambda v, g_=p["g"], q0=p["Q0"], q=p["QW"]:
                          v.tensor_tensor(
                              arnA[2 * g_ + 1][:, q0:q0 + q],
                              atmp[:, 512:512 + q],
                              psB[:, 1024:1024 + q], OP.mult), inc="V")
                return t_ev

            pend = None
            for gi, (Q0, QW, g, kts) in enumerate(groups):
                pv_started = [False] * 4
                z_started = [False] * 4
                t_pvz = None
                pv_gate = pend["t_cp"] if pend is not None else st["evacV"]
                for ki, kt in enumerate(kts):
                    par = st["ki"] % 2
                    st["ki"] += 1
                    psc = psA if par == 0 else psB
                    k0 = 128 * kt
                    qlo = max(k0 - 32, 32)
                    qhi = min(k0 + 160, QEND)
                    w0 = list(waits) + [st["exp"][par], t_consts_gd]
                    op("tensor", lambda t, p=psc: t.matmul(
                        p[:, 0:512], idn[:], mask4[:, 0:512], start=True, stop=True),
                        waits=tuple(w0))
                    op("tensor", lambda t, p=psc: t.matmul(
                        p[:, 512:768], idn[:], mask4[:, 512:768], start=True, stop=True))
                    def emit_score(j):
                        hh = 4 * g + j
                        ch = hh % 8
                        rb = 64 * (hh // 8)
                        cs = 192 * j + (qlo - (k0 - 32))
                        ce = cs + (qhi - qlo)
                        s0 = cs
                        while s0 < ce:
                            s1 = min(ce, ((s0 // 512) + 1) * 512)
                            sn = s1 - s0
                            op("tensor", lambda t, p=psc, c=ch, r=rb, k=k0, q0=qlo,
                               s=s0, m=sn, base=cs: t.matmul(
                                   p[:, s:s + m],
                                   arnK[c][r:r + 64, k:k + 128],
                                   arnQ[c][r:r + 64, q0 + (s - base):q0 + (s - base) + m],
                                   start=False, stop=True))
                            s0 = s1
                    for j in range(4):
                        emit_score(j)
                    t_sc = mark("tensor", "T")
                    if ki == 0 and pend is not None:
                        st["evacV"] = emit_tail(pend)
                        pend = None
                    st["exp"][par] = op("scalar", lambda s, p=psc, sl=par: s.activation(
                        ptb[:, 768 * sl:768 * sl + 768], p[:, 0:768], AF.Exp,
                        bias=0.0, scale=0.125), waits=(t_sc,), inc="S")
                    t_pv = emit_pv(g, kt, par, Q0, QW, qlo, qhi, k0,
                                   pv_started, st["exp"][par], pv_gate)
                    t_pvz = emit_z(g, kt, par, Q0, QW, qlo, qhi, k0,
                                   z_started, st["evacV"]) or t_pv
                t_cp0 = op("scalar", lambda s, q=QW: s.activation(
                    atmp[:, 0:q], psC[:, 0:q], AF.Copy, bias=0.0, scale=1.0),
                    waits=(t_pvz, st["evacV"]))
                t_cp = op("scalar", lambda s, q=QW: s.activation(
                    atmp[:, 512:512 + q], psC[:, 512:512 + q], AF.Copy,
                    bias=0.0, scale=1.0), inc="S")
                t_c0 = op("vec", lambda v, q=QW: v.tensor_scalar(
                    rztmp[:, 0:q], psA[:, 1024:1024 + q], 1e-30, None, OP.max),
                    waits=(t_pvz,), inc="V")
                op("scalar", lambda s, q=QW: s.activation(
                    rztmp[:, 0:q], rztmp[:, 0:q], AF.Ln,
                    bias=epsz[:, 0:1], scale=1.0), waits=(t_c0,))
                t_e0 = op("scalar", lambda s, q=QW: s.activation(
                    rzsb[0][:, 0:q], rztmp[:, 0:q], AF.Exp,
                    bias=0.0, scale=-1.0), inc="S")
                t_c1 = op("vec", lambda v, q=QW: v.tensor_scalar(
                    rztmp[:, 0:q], psB[:, 1024:1024 + q], 1e-30, None, OP.max),
                    waits=(t_e0,), inc="V")
                op("scalar", lambda s, q=QW: s.activation(
                    rztmp[:, 0:q], rztmp[:, 0:q], AF.Ln,
                    bias=epsz[:, 0:1], scale=1.0), waits=(t_c1,))
                t_r1 = op("scalar", lambda s, q=QW: s.activation(
                    rzsb[1][:, 0:q], rztmp[:, 0:q], AF.Exp,
                    bias=0.0, scale=-1.0), inc="S")
                pend = {"QW": QW, "g": g, "Q0": Q0, "t_r1": t_r1, "t_cp": t_cp}
            t_evac = emit_tail(pend)
            st["evacV"] = t_evac
            return t_evac

        def emit_pv(g, kt, par, Q0, QW, qlo, qhi, k0, pv_started, t_exp, pv_gate):
            clo = max(qlo, Q0)
            chi = min(qhi, Q0 + QW)
            if clo >= chi:
                return None
            cw = chi - clo
            first = True
            for j in range(4):
                hh = 4 * g + j
                bank = j // 2
                rb = 64 * (j % 2)
                ptoff = 768 * par + 192 * j + (qlo - (k0 - 32)) + (clo - qlo)
                stt = not pv_started[j]
                pv_started[j] = True
                op("tensor", lambda t, k=kt, h_=hh, b=bank, r=rb, po=ptoff, c0=clo,
                   m=cw, st_=stt: t.matmul(
                       psC[r:r + 64, 512 * b + (c0 - Q0):512 * b + (c0 - Q0) + m],
                       arnV[:, 1024 * k + 64 * h_:1024 * k + 64 * h_ + 64],
                       ptb[:, po:po + m], start=st_, stop=True),
                   waits=(t_exp, pv_gate) if first else ())
                first = False
            return mark("tensor", "T")

        def emit_z(g, kt, par, Q0, QW, qlo, qhi, k0, z_started, z_gate):
            clo = max(qlo, Q0)
            chi = min(qhi, Q0 + QW)
            if clo >= chi:
                return None
            cw = chi - clo
            zmap = ((psA, 0), (psA, 32), (psA, 64), (psB, 0))
            zfirst = True
            for j in range(4):
                pz, r = zmap[j]
                ptoff = 768 * par + 192 * j + (qlo - (k0 - 32)) + (clo - qlo)
                stt = not z_started[j]
                z_started[j] = True
                op("tensor", lambda t, p=pz, ro=r, po=ptoff, c0=clo, m=cw,
                   st_=stt: t.matmul(
                       p[ro:ro + 1, 1024 + (c0 - Q0):1024 + (c0 - Q0) + m],
                       onesb[:], ptb[:, po:po + m], start=st_, stop=True),
                   waits=(z_gate,) if (zfirst and stt) else ())
                zfirst = False
            return mark("tensor", "T")

        w_toks = {"wq": load_weight("wq", 0, ()), "wk": load_weight("wk", 1, ())}
        cross = {"final_V": None, "out_SD": None,
                 "ffnout_V": None, "xbf_next": None}

        def dump(arena, tok):
            for cc in range(NCC):
                op("gdma", lambda g, c=cc: g.dma_start(
                    out=dbg[:, KVA * c:KVA * c + KVA], in_=arena[c][:, 0:KVA]),
                    waits=(tok,), inc="GD")

        def dump1(ap, cols, tok):
            op("gdma", lambda g: g.dma_start(out=dbg[:, 0:cols], in_=ap),
               waits=(tok,), inc="GD")

        def run_halves():
          for h in range(2):
            _CURH[0] = h
            XL = HTOK * h

            PH[0] = f"h{h}:xbf"
            if h == 0:
                t_xbf = None
                xbf_toks = {}
                for bi, (c0, n) in enumerate(BLK_S):
                    nn = KVA - c0 if bi == len(BLK_S) - 1 else n
                    for cc in range(NCC):
                        t_xbf = op("gdma", lambda g, c=cc, a=c0, m=nn, X=XL:
                                   g.dma_start(
                                       out=arnA[c][:, a:a + m],
                                       in_=xTs[128 * c:128 * c + 128,
                                               X + a:X + a + m]), inc="GD")
                        xbf_toks[(cc, bi)] = (t_xbf,)
            else:
                t_xbf = cross["xbf_next"]
                xbf_toks = cross["xbf_toks"]
            t_x2ld = []
            for cc in range(NCC):
                t_x2ld.append(op("sdma", lambda s, c=cc, X=XL: s.dma_start(
                    out=x2[c][:, 32:1057],
                    in_=xTs[128 * c:128 * c + 128, X + 32:X + 1057]),
                    waits=(cross["out_SD"],) if cc == 0 else (), inc="SD"))

            sA = lambda c, a, m: arnA[c][:, a:a + m]
            sS = lambda c, a, m: arnS[c][:, a:a + m]
            sQ = lambda c, a, m: arnQ[c][:, a:a + m]
            sK = lambda c, a, m: arnK[c][:, a:a + m]
            sX2 = lambda c, a, m: x2[c][:, a:a + m]

            PH[0] = f"h{h}:LN1"
            t_st1 = ln_stats(sA, BLK_S, 0, "1",
                             (t_xbf, t_ones, cross["final_V"], t_consts_gd), sq_fn=sS)
            t_xs1, ap1 = ln_apply2(sA, arnS, arnA, BLK_S, 0, "1", (t_st1,))
            q_toks = {}
            for cc in range(NCC):
                q_toks[(cc, 0)] = (ap1[(0, cc)], ap1[(1, cc)])
                q_toks[(cc, 1)] = (ap1[(1, cc)],)
                q_toks[(cc, 2)] = (ap1[(1, cc)], ap1[(2, cc)])
            k_toks = {(cc, b): (ap1[(b, cc)],) for cc in range(NCC) for b in range(3)}
            if h == DBG_H and DBG == "xs1":
                dump(arnA, t_xs1)
            _ckpt(1)

            PH[0] = f"h{h}:qkv"
            def ev_q(oc, bi, c0, n, pp, w):
                return op("scalar", lambda s, o=oc, a=c0, m=n, b=bi, p=pp: s.activation(
                    arnQ[o][:, a:a + m], p[:, 512 * b:512 * b + m], AF.Identity,
                    bias=qb_s[:, o:o + 1], scale=1.0),
                    waits=tuple(w) + ((cross["final_V"],) if (oc == 0 and bi == 0) else ()),
                    inc="S" if bi == 2 else None)

            t_qmm, t_qe = proj_cm(0, w_toks["wq"], sA, BLK_Q, ev_q, (),
                                  src_toks=q_toks)

            def ev_k(oc, bi, c0, n, pp, w):
                return op("scalar", lambda s, o=oc, a=c0, m=n, b=bi, p=pp: s.activation(
                    arnK[o][:, a:a + m], p[:, 512 * b:512 * b + m], AF.Copy,
                    bias=0.0, scale=1.0),
                    waits=tuple(w) + ((cross["ffnout_V"],) if (oc == 0 and bi == 0) else ()),
                    inc="S" if bi == 2 else None)

            w_toks["wv"] = load_weight("wv", 0, (t_qmm,))
            t_kmm, t_ke = proj_cm(1, w_toks["wk"], sA, BLK_S, ev_k, (),
                                  bias_fold=kbrow, XL=XL, src_toks=k_toks)
            w_toks["wo"] = load_weight("wo", 1, (t_kmm,))
            t_vmm, t_ve = proj_tm(w_toks["wv"], (), XL, src_toks=k_toks)
            w_toks["wr"] = load_weight("wr", 0, (t_vmm,))
            if h == DBG_H and DBG == "q":
                dump(arnQ, t_qe)
            if h == DBG_H and DBG == "k":
                dump(arnK, t_ke)
            if h == DBG_H and DBG == "v":
                for _kt in range(NKT):
                    op("gdma", lambda g, k=_kt: g.dma_start(
                        out=dbg[:, 1024 * k:1024 * k + 1024],
                        in_=arnV[:, 1024 * k:1024 * k + 1024]),
                        waits=(t_ve,), inc="GD")
            _ckpt(2)

            PH[0] = f"h{h}:attn"
            t_attnT = attention((t_qe, t_ke, t_ve))
            if h == DBG_H and DBG == "attnT":
                dump(arnA, t_attnT)
            _ckpt(3)

            PH[0] = f"h{h}:oproj"
            otoks = {}

            def ev_o(oc, bi, c0, n, pp, w):
                tok = op("scalar", lambda s, o=oc, a=c0, m=n, b=bi, p=pp: s.activation(
                    arnK[o][:, a:a + m], p[:, 512 * b:512 * b + m], AF.Identity,
                    bias=ob_s[:, o:o + 1], scale=1.0),
                    waits=w, inc="S" if bi == 2 else None)
                if bi == 2:
                    otoks[oc] = tok
                return tok

            t_omm, t_oe = proj_cm(1, w_toks["wo"], sA, BLK_Q, ev_o, (t_attnT,))
            w_toks["wk2"] = load_weight("wk2", 1, (t_omm,))
            if h == 0:
                pre = None
                pre_toks = {}
                for cc in range(NCC):
                    pre = op("gdma", lambda g, c=cc: g.dma_start(
                        out=arnA[c][:, 0:KVA],
                        in_=xTs[128 * c:128 * c + 128, HTOK:HTOK + KVA]),
                        waits=(t_omm,) if cc == 0 else (), inc="GD")
                    for b in range(3):
                        pre_toks[(cc, b)] = (pre,)
                cross["xbf_next"] = pre
                cross["xbf_toks"] = pre_toks
            if h == DBG_H and DBG == "attnout":
                dump(arnK, t_oe)
            _ckpt(4)

            PH[0] = f"h{h}:LNa"
            t_sta = ln_stats(sK, BLK_Q, 32, "a", (t_oe,), sq_fn=sS)
            if h == DBG_H and DBG == "rowsa":
                op("gdma", lambda g: g.dma_start(out=dbg[0:1, 0:KVA],
                                                 in_=rMS["a"][0:1, 0:KVA]),
                   waits=(t_sta,), inc="GD")
                op("gdma", lambda g: g.dma_start(out=dbg[1:2, 0:KVA],
                                                 in_=rA["a"][0:1, 0:KVA]),
                   waits=(t_sta,), inc="GD")
            pi = 0
            t_x2 = None
            x2toks = {}
            bctok = [None, None]
            for bi, (c0, n) in enumerate(BLK_Q):
                pb = psC if bi % 2 == 0 else psA
                o0 = c0 - 32
                w0 = [t_sta] + ([bctok[bi % 2]] if bctok[bi % 2] else [])
                t_bca = op("tensor", lambda t, p=pb, o=o0, m=n: t.matmul(
                    p[:, 0:m], ones1[:], rA["a"][0:1, o:o + m], start=True, stop=True),
                    waits=tuple(w0), inc="T")
                for cc in range(NCC):
                    op("vec", lambda v, c=cc, a=c0, m=n, sl=pi % 2, p=pb: v.tensor_tensor(
                        xstage[:, 512 * sl:512 * sl + m], arnK[c][:, a:a + m],
                        p[:, 0:m], OP.mult),
                        waits=(t_bca,) if cc == 0 else ())
                    t_x2 = op("vec", lambda v, c=cc, a=c0, m=n, sl=pi % 2: v.tensor_tensor(
                        x2[c][:, a:a + m], x2[c][:, a:a + m],
                        xstage[:, 512 * sl:512 * sl + m], OP.add),
                        waits=(t_x2ld[-1],), inc="V")
                    x2toks[(bi, cc)] = t_x2
                    pi += 1
                bctok[bi % 2] = t_x2

            if h == DBG_H and DBG == "x2":
                for cc in range(NCC):
                    op("gdma", lambda g, c=cc: g.dma_start(
                        out=dbg[:, KVA * c:KVA * c + KVA], in_=x2[c][:, 0:KVA]),
                        waits=(t_x2,), inc="GD")
            _ckpt(5)
            PH[0] = f"h{h}:LN2"
            cast_toks = {}
            for cc in range(NCC):
                for bi, (c0, n) in enumerate(BLK_Q):
                    cast_toks[(cc, bi)] = (op(
                        "scalar", lambda s, c=cc, a=c0, m=n: s.activation(
                            arnQ[c][:, a:a + m], x2[c][:, a:a + m], AF.Copy,
                            bias=0.0, scale=1.0), waits=(x2toks[(bi, cc)],),
                        inc="S"),)
            t_st2 = ln_stats(sQ, BLK_Q, 32, "2", (t_x2,), sq_fn=sS,
                             src_toks=cast_toks)
            t_a2m = op("vec", lambda v, X=XL: v.tensor_tensor(
                rA["2"][0:1, 0:1025], rA["2"][0:1, 0:1025],
                kvm[0:1, X + 32:X + 32 + 1025], OP.mult), waits=(t_st2,), inc="V")
            t_ln2, _ap2 = ln_apply2(sX2, arnQ, arnQ, BLK_Q, 32, "2", (t_a2m,))

            if h == DBG_H and DBG == "ln2":
                dump(arnQ, t_ln2)
            _ckpt(6)
            PH[0] = f"h{h}:mix"
            t_mix = None
            mix_toks = {}
            for cc in range(NCC):
                for (c0, n) in BLK_O:
                    op("vec", lambda v, c=cc, a=c0, m=n: v.tensor_tensor(
                        arnK[c][:, a:a + m], arnQ[c][:, a:a + m],
                        arnQ[c][:, a - 1:a - 1 + m], OP.subtract),
                        waits=(t_ln2,) if cc == 0 else ())
                for (c0, n) in BLK_O:
                    op("vec", lambda v, c=cc, a=c0, m=n: v.scalar_tensor_tensor(
                        arnV[:, 1152 * c + a:1152 * c + a + m], arnK[c][:, a:a + m],
                        mur_s[:, c:c + 1], arnQ[c][:, a - 1:a - 1 + m],
                        OP.mult, OP.add), waits=())
                for (c0, n) in BLK_O:
                    t_mix = op("vec", lambda v, c=cc, a=c0, m=n: v.scalar_tensor_tensor(
                        arnS[c][:, a:a + m], arnK[c][:, a:a + m],
                        muk_s[:, c:c + 1], arnQ[c][:, a - 1:a - 1 + m],
                        OP.mult, OP.add), waits=(),
                        inc="V" if c0 == 545 else None)
                for b in range(2):
                    mix_toks[(cc, b)] = (t_mix,)

            if h == DBG_H and DBG == "xr":
                dump1(arnV[:, 0:8 * KVA], 8 * KVA, t_mix)
            if h == DBG_H and DBG == "xk":
                dump(arnS, t_mix)
            _ckpt(7)
            PH[0] = f"h{h}:ffn"
            sXR = lambda c, a, m: arnV[:, 1152 * c + a:1152 * c + a + m]

            def ev_r(oc, bi, c0, n, pp, w):
                return op("scalar", lambda s, o=oc, a=c0, m=n, b=bi, p=pp: s.activation(
                    arnK[o][:, a:a + m], p[:, 512 * b:512 * b + m], AF.Sigmoid,
                    bias=0.0, scale=1.0),
                    waits=tuple(w) + ((t_mix,) if (oc == 0 and bi == 0) else ()),
                    inc="S" if bi == len(BLK_O) - 1 else None)

            t_rmm, t_re = proj_cm(0, w_toks["wr"], sXR, BLK_O, ev_r, (),
                                  src_toks=mix_toks)
            w_toks["wv2"] = load_weight("wv2", 0, (t_rmm,))

            def ev_kr(oc, bi, c0, n, pp, w):
                return op("scalar", lambda s, o=oc, a=c0, m=n, b=bi, p=pp: s.activation(
                    arnV[:, 1152 * o + a:1152 * o + a + m],
                    p[:, 512 * b:512 * b + m], AF.Relu,
                    bias=0.0, scale=1.0), waits=w,
                    inc="S" if bi == len(BLK_O) - 1 else None)

            t_k2mm, t_k2e = proj_cm(1, w_toks["wk2"], sS, BLK_O, ev_kr, (),
                                    src_toks=mix_toks)
            if h == 0:
                w_toks["wk_next"] = load_weight("wk", 1, (t_k2mm,))
            t_sqk = None
            for cc in range(NCC):
                for (c0, n) in BLK_O:
                    t_sqk = op("vec", lambda v, c=cc, a=c0, m=n: v.tensor_tensor(
                        arnV[:, 1152 * c + a:1152 * c + a + m],
                        arnV[:, 1152 * c + a:1152 * c + a + m],
                        arnV[:, 1152 * c + a:1152 * c + a + m],
                        OP.mult), waits=(t_k2e,),
                        inc="V" if (cc == NCC - 1 and c0 == 545) else None)

            ftoks = {}

            def ev_f(oc, bi, c0, n, pp, w):
                tok = op("vec", lambda v, o=oc, a=c0, m=n, b=bi, p=pp: v.tensor_tensor(
                    arnQ[o][:, a:a + m], p[:, 512 * b:512 * b + m], arnK[o][:, a:a + m],
                    OP.mult), waits=tuple(w) + ((t_re,) if (oc == 0 and bi == 0) else ()),
                    inc="V" if bi == len(BLK_O) - 1 else None)
                if bi == len(BLK_O) - 1:
                    ftoks[oc] = tok
                return tok

            t_v2mm, t_fe = proj_cm(0, w_toks["wv2"], sXR, BLK_O, ev_f, (t_sqk,))
            cross["ffnout_V"] = t_fe
            if h == 0:
                w_toks["wq_next"] = load_weight("wq", 0, (t_v2mm,))

            PH[0] = f"h{h}:LNfout"
            if h == DBG_H and DBG == "r":
                dump(arnK, t_re)
            if h == DBG_H and DBG == "kact":
                dump(arnA, t_sqk)
            if h == DBG_H and DBG == "ffnout":
                dump(arnQ, t_fe)
            _ckpt(8)
            t_stf = ln_stats(sQ, BLK_O, 33, "f", (t_fe,), sq_fn=sS, want_ms=False)
            op("vec", lambda v: v.tensor_tensor(
                rT1[0:1, 0:1024], rMS["a"][0:1, 1:1025], rSQ[0:1, 0:1024], OP.mult),
                waits=(t_stf,))
            t_d = op("vec", lambda v: v.tensor_tensor(
                rD[0:1, 0:1024], rT1[0:1, 0:1024], psA[0:1, 0:1024], OP.add), inc="V")

            if h == DBG_H and DBG == "rows":
                op("gdma", lambda g: g.dma_start(out=dbg[0:1, 0:KVA],
                                                 in_=rA["f"][0:1, 0:KVA]),
                   waits=(t_d,), inc="GD")
                op("gdma", lambda g: g.dma_start(out=dbg[1:2, 0:KVA],
                                                 in_=rD[0:1, 0:KVA]),
                   waits=(t_d,), inc="GD")
                op("gdma", lambda g: g.dma_start(out=dbg[2:3, 0:KVA],
                                                 in_=rMS["a"][0:1, 0:KVA]),
                   waits=(t_d,), inc="GD")
            t_out = None
            bctok2 = [None, None]
            t_fin = None
            for bi, (c0, n) in enumerate(BLK_O):
                pb = psC if bi % 2 == 0 else psA
                o0 = c0 - 33
                w0 = [t_d] + ([bctok2[bi % 2]] if bctok2[bi % 2] else [])
                op("tensor", lambda t, p=pb, o=o0, m=n: t.matmul(
                    p[:, 0:m], onesdiv[:], rD[0:1, o:o + m], start=True, stop=True),
                    waits=tuple(w0))
                t_bcf = op("tensor", lambda t, p=pb, o=o0, m=n: t.matmul(
                    p[:, 512:512 + m], ones1[:], rA["f"][0:1, o:o + m],
                    start=True, stop=True), inc="T")
                for cc in range(NCC):
                    op("vec", lambda v, c=cc, a=c0, m=n, p=pb: v.tensor_tensor(
                        bstage[:, 0:m], arnQ[c][:, a:a + m], p[:, 0:m], OP.subtract),
                        waits=(t_bcf,) if cc == 0 else ())
                    op("vec", lambda v, c=cc, a=c0, m=n, p=pb: v.tensor_tensor(
                        xstage[:, 0:m], bstage[:, 0:m], p[:, 512:512 + m], OP.mult),
                        waits=())
                    t_fin = op("vec", lambda v, c=cc, a=c0, m=n: v.tensor_tensor(
                        x2[c][:, a:a + m], xstage[:, 0:m], x2[c][:, a:a + m], OP.add),
                        inc="V")
                    t_out = op("sdma", lambda s, c=cc, a=c0, m=n, hh=h: s.dma_start(
                        out=yT[128 * c:128 * c + 128,
                               HTOK * hh + a - 33:HTOK * hh + a - 33 + m],
                        in_=x2[c][:, a:a + m]), waits=(t_fin,), inc="SD")
                bctok2[bi % 2] = t_fin
            cross["final_V"] = t_fin
            cross["out_SD"] = t_out
            if h == 0:
                w_toks["wq"] = w_toks.pop("wq_next")
                w_toks["wk"] = w_toks.pop("wk_next")
                _ckpt(9)

        try:
            run_halves()
        except _StopBuild:
            pass

        waited = {}

        def emit_stream(eng_obj, eng_name):
            for (e, builder, ws, inc, amt, _phl) in OPS:
                if e != eng_name:
                    continue
                for (sname, val) in ws:
                    key = (eng_name, sname)
                    if waited.get(key, -1) >= val:
                        continue
                    waited[key] = val
                    eng_obj.wait_ge(sems[sname], val)
                ins = builder(eng_obj)
                if inc is not None:
                    ins.then_inc(sems[inc], amt)

        @block.tensor
        def _(t):
            emit_stream(t, "tensor")

        @block.vector
        def _(v):
            emit_stream(v, "vec")

        @block.scalar
        def _(s):
            emit_stream(s, "scalar")

        @block.gpsimd
        def _(g):
            emit_stream(g, "gdma")
            g.wait_ge(sems["GD"], CNT["GD"])

        @block.sync
        def _(s):
            emit_stream(s, "sdma")
            s.wait_ge(sems["SD"], CNT["SD"])

        _BUILD_LOG["ops"] = [(e, tuple(ws), inc, phl)
                             for (e, b, ws, inc, amt, phl) in OPS]
    return nc


def _host_prep(inputs):
    x = np.asarray(inputs["L0_attn_hidden_states"], dtype=np.float32).reshape(SQ, CM)
    xT = np.ascontiguousarray(x.T)
    XP = np.zeros((CM, 33 + SQ + (XCOLS - 33 - TOK)), dtype=np.float32)
    XP = np.zeros((CM, SQ + XCOLS), dtype=np.float32)
    XP[:, 33:33 + SQ] = xT
    kvm_full = np.zeros((1, SQ + XCOLS), dtype=np.float32)
    kvm_full[0, 33:33 + SQ] = 1.0

    def wt(n):
        return np.ascontiguousarray(np.asarray(inputs[n], dtype=np.float32).T)

    perm = np.concatenate([np.r_[64 * c:64 * c + 64, 64 * (c + 8):64 * (c + 8) + 64]
                           for c in range(8)])
    w = {
        "wq": np.ascontiguousarray(wt("L0_attn_q_w")[:, perm]),
        "wk": np.ascontiguousarray(wt("L0_attn_k_w")[:, perm]),
        "wv": wt("L0_attn_v_w"),
        "wo": wt("L0_attn_o_w"), "wr": wt("L0_ffn_W_r"), "wk2": wt("L0_ffn_W_k"),
        "wv2": wt("L0_ffn_W_v"),
    }

    def col8(n):
        return np.ascontiguousarray(
            np.asarray(inputs[n], dtype=np.float32).reshape(8, 128).T)

    qb8 = np.ascontiguousarray(
        np.asarray(inputs["L0_attn_q_b"], dtype=np.float32)[perm].reshape(8, 128).T)
    ob8 = col8("L0_attn_o_b")
    mur8 = col8("L0_ffn_mu_r")
    muk8 = col8("L0_ffn_mu_k")
    kbrow = np.asarray(inputs["L0_attn_k_b"], dtype=np.float32)[perm].reshape(1, CM)
    vbrow = np.asarray(inputs["L0_attn_v_b"], dtype=np.float32).reshape(1, CM)

    kk = np.arange(128)[:, None]
    qq = np.arange(192)[None, :]
    band = ((qq - kk >= 0) & (qq - kk <= 64))
    m1 = np.where(band, 0.0, MASKVAL).astype(np.float32)
    mask4 = np.ascontiguousarray(np.tile(m1, (1, 4)))
    idn = np.eye(128, dtype=np.float32)

    in_maps = []
    for c in range(8):
        s = TOK * c
        im = {
            "xTs": np.ascontiguousarray(XP[:, s:s + XCOLS]),
            "kvm": np.ascontiguousarray(kvm_full[:, s:s + XCOLS]),
            "qb8": qb8, "ob8": ob8, "mur8": mur8, "muk8": muk8,
            "kbrow": kbrow, "vbrow": vbrow, "mask4": mask4, "idn": idn,
        }
        im.update(w)
        in_maps.append(im)
    return in_maps


def kernel(**inputs):
    in_maps = _host_prep(inputs)
    if "nc" not in _NC_CACHE:
        _NC_CACHE["nc"] = build_program()
    nc = _NC_CACHE["nc"]
    res = run_bass_kernel_spmd(nc, in_maps, core_ids=list(range(8)))
    yT_full = np.concatenate([np.asarray(r["yT"], dtype=np.float32)
                              for r in res.results], axis=1)
    out = np.ascontiguousarray(yT_full.T).reshape(1, SQ, CM).astype(np.float32)
    return out


# version 29
# speedup vs baseline: 1.0213x; 1.0213x over previous
import numpy as np
import concourse.bass as bass
import concourse.mybir as mybir
from concourse.bass_utils import run_bass_kernel_spmd
from concourse.alu_op_type import AluOpType as OP
from contextlib import ExitStack

F32 = mybir.dt.float32
BF16 = mybir.dt.bfloat16
AF = mybir.ActivationFunctionType

SQ = 16384
CM = 1024
NCC = 8
H = 16
HD = 64
TOK = 2048
HTOK = 1024
KVA = 1152
XCOLS = 2176
NKT = 9
MASKVAL = -1e30

BLK_S = [(0, 512), (512, 512), (1024, 128)]
BLK_Q = [(32, 512), (544, 257), (801, 256)]
BLK_O = [(33, 512), (545, 512)]
QEND = 1057

WNAMES = ["wq", "wk", "wv", "wo", "wr", "wk2", "wv2"]

_NC_CACHE = {}
_BUILD_LOG = {}
PH = [""]
STOP_PHASE = 99
DISABLE = set()
DBG = None
DBG_H = 0
DBG_G = 0


class _StopBuild(Exception):
    pass


_CURH = [0]


def _ckpt(n):
    if STOP_PHASE <= n + 10 * _CURH[0]:
        raise _StopBuild()


def build_program():
    nc = bass.Bass("TRN2", target_bir_lowering=False)

    xTs = nc.dram_tensor("xTs", [CM, XCOLS], F32, kind="ExternalInput").ap()
    wd = {n: nc.dram_tensor(n, [CM, CM], F32, kind="ExternalInput").ap() for n in WNAMES}
    qb8_d = nc.dram_tensor("qb8", [128, 8], F32, kind="ExternalInput").ap()
    ob8_d = nc.dram_tensor("ob8", [128, 8], F32, kind="ExternalInput").ap()
    mur8_d = nc.dram_tensor("mur8", [128, 8], F32, kind="ExternalInput").ap()
    muk8_d = nc.dram_tensor("muk8", [128, 8], F32, kind="ExternalInput").ap()
    kbrow_d = nc.dram_tensor("kbrow", [1, CM], F32, kind="ExternalInput").ap()
    vbrow_d = nc.dram_tensor("vbrow", [1, CM], F32, kind="ExternalInput").ap()
    kvm_d = nc.dram_tensor("kvm", [1, XCOLS], F32, kind="ExternalInput").ap()
    mask_d = nc.dram_tensor("mask4", [128, 768], F32, kind="ExternalInput").ap()
    idn_d = nc.dram_tensor("idn", [128, 128], F32, kind="ExternalInput").ap()
    yT = nc.dram_tensor("yT", [CM, TOK], F32, kind="ExternalOutput").ap()
    dbg = nc.dram_tensor("dbg", [128, NKT * 1024], BF16, kind="ExternalOutput").ap()

    OPS = []
    CNT = {k: 0 for k in ("T", "V", "S", "GD", "SD")}
    TOKENG = {}

    def op(eng, builder, waits=(), inc=None, amt=None):
        tok = None
        if inc is not None:
            if amt is None:
                amt = 16 if inc in ("GD", "SD") else 1
            CNT[inc] += amt
            tok = (inc, CNT[inc])
            TOKENG[tok] = eng
        ws = []
        for t in waits:
            if t is None:
                continue
            if TOKENG.get(t) == eng and t[0] not in ("GD", "SD"):
                continue
            ws.append(t)
        OPS.append((eng, builder, ws, inc, amt, PH[0]))
        return tok

    def mark(eng, inc, amt=None):
        if amt is None:
            amt = 16 if inc in ("GD", "SD") else 1
        for i in range(len(OPS) - 1, -1, -1):
            if OPS[i][0] == eng:
                e, b, ws, oldinc, oldamt, phl = OPS[i]
                assert oldinc is None, "op already carries an inc"
                CNT[inc] += amt
                tok = (inc, CNT[inc])
                TOKENG[tok] = eng
                OPS[i] = (e, b, ws, inc, amt, phl)
                return tok
        raise RuntimeError("no op to mark")

    with ExitStack() as stk:
        en = stk.enter_context
        arnA = [en(nc.sbuf_tensor(f"arnA{i}", [128, KVA], BF16)) for i in range(NCC)]
        arnS = [en(nc.sbuf_tensor(f"arnS{i}", [128, KVA], BF16)) for i in range(NCC)]
        arnQ = [en(nc.sbuf_tensor(f"arnQ{i}", [128, KVA], BF16)) for i in range(NCC)]
        arnK = [en(nc.sbuf_tensor(f"arnK{i}", [128, KVA], BF16)) for i in range(NCC)]
        arnV = en(nc.sbuf_tensor("arnV", [128, NKT * 1024], BF16))
        x2 = [en(nc.sbuf_tensor(f"x2_{i}", [128, KVA], F32)) for i in range(NCC)]
        ptb = en(nc.sbuf_tensor("ptb", [128, 2 * 768], BF16))
        wbuf = [en(nc.sbuf_tensor(f"wbuf{i}", [128, NCC * 1024], BF16)) for i in range(2)]
        rzsb = [en(nc.sbuf_tensor(f"rzsb{i}", [128, 512], BF16)) for i in range(2)]
        rztmp = en(nc.sbuf_tensor("rztmp", [128, 512], F32))
        xstage = en(nc.sbuf_tensor("xstage", [128, 1024], F32))
        bstage = en(nc.sbuf_tensor("bstage", [128, 512], BF16))
        atmp = en(nc.sbuf_tensor("atmp", [128, 1024], BF16))
        rT1 = en(nc.sbuf_tensor("rT1", [1, KVA], F32))
        rSQ = en(nc.sbuf_tensor("rSQ", [1, KVA], F32))
        rA = {k: en(nc.sbuf_tensor(f"rA_{k}", [1, KVA], BF16)) for k in ("1", "a", "f")}
        rA["2"] = rA["1"]
        rMS = {k: en(nc.sbuf_tensor(f"rMS_{k}", [1, KVA], BF16)) for k in ("1", "a")}
        rMS["2"] = rMS["1"]
        rD = en(nc.sbuf_tensor("rD", [1, KVA], BF16))
        onesb = en(nc.sbuf_tensor("onesb", [128, 1], BF16))
        epsb = en(nc.sbuf_tensor("epsb", [1, 1], F32))
        epsz = en(nc.sbuf_tensor("epsz", [128, 1], F32))
        sel = {r: en(nc.sbuf_tensor(f"sel{r}", [128, 64], BF16)) for r in (0, 32, 64)}
        ones1 = en(nc.sbuf_tensor("ones1", [1, 128], BF16))
        onesdiv = en(nc.sbuf_tensor("onesdiv", [1, 128], BF16))
        qb_s = en(nc.sbuf_tensor("qb_s", [128, 8], F32))
        ob_s = en(nc.sbuf_tensor("ob_s", [128, 8], F32))
        mur_s = en(nc.sbuf_tensor("mur_s", [128, 8], F32))
        muk_s = en(nc.sbuf_tensor("muk_s", [128, 8], F32))
        kbrow = en(nc.sbuf_tensor("kbrow_s", [1, CM], BF16))
        vbrow = en(nc.sbuf_tensor("vbrow_s", [1, CM], BF16))
        kvm = en(nc.sbuf_tensor("kvm_s", [1, XCOLS], BF16))
        mask4 = en(nc.sbuf_tensor("mask4_s", [128, 768], BF16))
        idn = en(nc.sbuf_tensor("idn_s", [128, 128], BF16))
        psA = en(nc.psum_tensor("psA", [128, 1536], F32))
        psB = en(nc.psum_tensor("psB", [128, 1536], F32))
        psC = en(nc.psum_tensor("psC", [128, 1024], F32))
        sems = {k: en(nc.semaphore(f"sem{k}")) for k in ("T", "V", "S", "GD", "SD")}
        en(nc.allow_low_precision(
            "bf16 row normalizers / bf16 matmul operands by design; all "
            "accumulation stays fp32 in PSUM"))
        block = en(nc.Block())

        for dst, src in ((qb_s, qb8_d), (ob_s, ob8_d), (mur_s, mur8_d), (muk_s, muk8_d)):
            op("sdma", lambda s, d=dst, so=src: s.dma_start(out=d[:], in_=so), inc="SD")
        gl = []
        for dst, src in ((kbrow, kbrow_d), (vbrow, vbrow_d), (kvm, kvm_d),
                         (mask4, mask_d), (idn, idn_d)):
            gl.append(op("gdma", lambda g, d=dst, so=src: g.dma_start(out=d[:], in_=so),
                         inc="GD"))
        t_consts_gd = gl[-1]
        op("vec", lambda v: v.memset(onesb[:], 1.0))
        op("vec", lambda v: v.memset(epsb[:], 1e-5))
        op("vec", lambda v: v.memset(epsz[:], 1e-30))
        op("vec", lambda v: v.memset(ones1[:], 1.0))
        for r in (0, 32, 64):
            op("vec", lambda v, rr=r: v.memset(sel[rr][:, 0:64], 0.0))
            op("vec", lambda v, rr=r: v.memset(sel[rr][rr:rr + 1, 0:64], 1.0))
        op("vec", lambda v: v.memset(atmp[:, 0:512], 1.0))
        t_ones = op("vec", lambda v: v.memset(onesdiv[:], 1.0 / 1024.0), inc="V")
        op("tensor", lambda t: t.matmul(psA[:, 1024:1536], idn[:], atmp[:, 0:512],
                                        start=True, stop=True),
           waits=(t_ones, t_consts_gd))
        op("tensor", lambda t: t.matmul(psB[:, 1024:1536], idn[:], atmp[:, 0:512],
                                        start=True, stop=True))

        def load_weight(name, bufi, waits):
            last = None
            for cc in range(NCC):
                last = op("gdma",
                          lambda g, b=wbuf[bufi], c=cc, n=name: g.dma_start(
                              out=b[:, 1024 * c:1024 * c + 1024],
                              in_=wd[n][128 * c:128 * c + 128, :]),
                          waits=waits if cc == 0 else (), inc="GD")
            return last

        def ln_stats(src_fn, blocks, shift, key, waits, sq_fn, want_ms=True,
                     src_toks=None):
            first = True
            last = None
            for bi, (c0, n) in enumerate(blocks):
                o0 = c0 - shift
                for cc in range(NCC):
                    ws = waits if first else ()
                    if src_toks is not None:
                        ws = tuple(ws) + tuple(src_toks.get((cc, bi), ()))
                    last = op("tensor",
                              lambda t, c=cc, a=c0, m=n, o=o0, st=(cc == 0): t.matmul(
                                  psA[0:1, o:o + m], onesb[:], src_fn(c, a, m),
                                  start=st, stop=True),
                              waits=ws)
                    first = False
            t_smm = mark("tensor", "T")
            last = None
            for cc in range(NCC):
                for bi, (c0, n) in enumerate(blocks):
                    ws = waits
                    if src_toks is not None:
                        ws = tuple(ws) + tuple(src_toks.get((cc, bi), ()))
                    last = op("vec", lambda v, c=cc, a=c0, m=n: v.tensor_tensor(
                        sq_fn(c, a, m), src_fn(c, a, m), src_fn(c, a, m), OP.mult),
                        waits=ws)
            t_sq = op("vec", lambda v: v.memset(rT1[0:1, KVA - 1:KVA], 0.0), inc="V")
            first = True
            for (c0, n) in blocks:
                o0 = c0 - shift
                for cc in range(NCC):
                    op("tensor", lambda t, c=cc, a=c0, m=n, o=o0, st=(cc == 0): t.matmul(
                        psB[0:1, o:o + m], onesb[:], sq_fn(c, a, m), start=st, stop=True),
                        waits=(t_sq,) if first else ())
                    first = False
            t_qmm = mark("tensor", "T")
            W = blocks[-1][0] + blocks[-1][1] - shift
            t_s2 = op("scalar", lambda s: s.activation(
                rT1[0:1, 0:W], psA[0:1, 0:W], AF.Square, bias=0.0, scale=1.0),
                waits=(t_smm,), inc="S")
            t_u = op("vec", lambda v: v.scalar_tensor_tensor(
                rT1[0:1, 0:W], psB[0:1, 0:W], 1024.0, rT1[0:1, 0:W],
                OP.mult, OP.subtract), waits=(t_s2, t_qmm), inc="V")
            op("scalar", lambda s: s.activation(
                rT1[0:1, 0:W], rT1[0:1, 0:W], AF.Ln, bias=epsb[0:1, 0:1],
                scale=2.0 ** -20), waits=(t_u,))
            t_a = op("scalar", lambda s: s.activation(
                rA[key][0:1, 0:W], rT1[0:1, 0:W], AF.Exp, bias=0.0,
                scale=-0.5), inc="S")
            if want_ms:
                return op("vec", lambda v: v.tensor_tensor(
                    rMS[key][0:1, 0:W], psA[0:1, 0:W], rA[key][0:1, 0:W], OP.mult),
                    waits=(t_a,), inc="V")
            return op("scalar", lambda s: s.activation(
                rSQ[0:1, 0:W], rT1[0:1, 0:W], AF.Exp, bias=0.0,
                scale=0.5), inc="S")

        def ln_apply2(src_fn, mid, dst, blocks, shift, key, waits):
            prevtok = [None, None]
            last = None
            tokmap = {}
            for bi, (c0, n) in enumerate(blocks):
                o0 = c0 - shift
                pb = psC if bi % 2 == 0 else psA
                w0 = list(waits) + ([prevtok[bi % 2]] if prevtok[bi % 2] else [])
                op("tensor", lambda t, p=pb, o=o0, m=n: t.matmul(
                    p[:, 0:m], onesdiv[:], rMS[key][0:1, o:o + m], start=True, stop=True),
                    waits=tuple(w0))
                tb = op("tensor", lambda t, p=pb, o=o0, m=n: t.matmul(
                    p[:, 512:512 + m], ones1[:], rA[key][0:1, o:o + m],
                    start=True, stop=True), inc="T")
                for cc in range(NCC):
                    op("vec", lambda v, c=cc, a=c0, m=n, p=pb: v.tensor_tensor(
                        mid[c][:, a:a + m], src_fn(c, a, m), p[:, 0:m], OP.subtract),
                        waits=(tb,) if cc == 0 else ())
                for cc in range(NCC):
                    tokmap[(bi, cc)] = op(
                        "vec", lambda v, c=cc, a=c0, m=n, p=pb: v.tensor_tensor(
                            dst[c][:, a:a + m], mid[c][:, a:a + m], p[:, 512:512 + m],
                            OP.mult), waits=(), inc="V")
                last = op("vec", lambda v: v.memset(rT1[0:1, KVA - 1:KVA], 0.0), inc="V")
                prevtok[bi % 2] = last
            return last, tokmap

        def proj_cm(bufi, w_tok, src_fn, blocks, evac, waits, bias_fold=None, XL=0,
                    src_toks=None):
            evac_tok = [None, None]
            t_mm_last = None
            last_ev = None
            for oc in range(NCC):
                pp = psA if oc % 2 == 0 else psB
                w0 = [w_tok] + list(waits) + ([evac_tok[oc % 2]] if evac_tok[oc % 2] else [])
                first = True
                for bi, (c0, n) in enumerate(blocks):
                    for cc in range(NCC):
                        ws = tuple(w0) if first else ()
                        if src_toks is not None:
                            ws = ws + tuple(src_toks.get((cc, bi), ()))
                        op("tensor", lambda t, p=pp, c=cc, o=oc, a=c0, m=n, b=bi,
                           st=(cc == 0), bf=bufi: t.matmul(
                               p[:, 512 * b:512 * b + m],
                               wbuf[bf][:, 1024 * c + 128 * o:1024 * c + 128 * o + 128],
                               src_fn(c, a, m), start=st, stop=True),
                           waits=ws)
                        first = False
                if bias_fold is not None:
                    brow = bias_fold
                    for bi, (c0, n) in enumerate(blocks):
                        op("tensor", lambda t, p=pp, o=oc, a=c0, m=n, b=bi: t.matmul(
                            p[:, 512 * b:512 * b + m],
                            brow[0:1, 128 * o:128 * o + 128],
                            kvm[0:1, XL + a:XL + a + m], start=False, stop=True))
                t_mm_last = mark("tensor", "T")
                for bi, (c0, n) in enumerate(blocks):
                    last_ev = evac(oc, bi, c0, n, pp, (t_mm_last,) if bi == 0 else ())
                evac_tok[oc % 2] = last_ev
            return t_mm_last, last_ev

        def proj_tm(w_tok, waits, XL, src_toks=None):
            evac_tok = [None, None]
            t_mm = None
            last_ev = None
            for tt in range(NKT):
                pp = psA if tt % 2 == 0 else psB
                w0 = [w_tok] + list(waits) + ([evac_tok[tt % 2]] if evac_tok[tt % 2] else [])
                first = True
                bt = min(tt // 4, 2)
                for cc in range(NCC):
                    for dv in range(2):
                        ws = tuple(w0) if first else ()
                        if src_toks is not None and dv == 0:
                            ws = ws + tuple(src_toks.get((cc, bt), ()))
                        op("tensor", lambda t, p=pp, c=cc, k=tt, d=dv, st=(cc == 0):
                           t.matmul(p[:, 512 * d:512 * d + 512],
                                    arnA[c][:, 128 * k:128 * k + 128],
                                    wbuf[0][:, 1024 * c + 512 * d:1024 * c + 512 * d + 512],
                                    start=st, stop=True),
                           waits=ws)
                        first = False
                for dv in range(2):
                    op("tensor", lambda t, p=pp, k=tt, d=dv: t.matmul(
                        p[:, 512 * d:512 * d + 512],
                        kvm[0:1, XL + 128 * k:XL + 128 * k + 128],
                        vbrow[0:1, 512 * d:512 * d + 512], start=False, stop=True))
                t_mm = mark("tensor", "T")
                last_ev = op("scalar", lambda s, p=pp, k=tt: s.activation(
                    arnV[:, 1024 * k:1024 * k + 1024], p[:, 0:1024], AF.Copy,
                    bias=0.0, scale=1.0), waits=(t_mm,), inc="S")
                evac_tok[tt % 2] = last_ev
            return t_mm, last_ev

        att_state = {"exp": [None, None], "msk": [None, None], "ki": 0,
                     "evacV": None, "evacT": None}

        def attention(waits):
            st = att_state
            groups = []
            for (Q0, QW) in BLK_Q:
                kts = [kt for kt in range(NKT)
                       if max(128 * kt - 32, 32) < Q0 + QW
                       and min(128 * kt + 160, QEND) > Q0]
                for g in range(4):
                    groups.append((Q0, QW, g, kts))

            def emit_tail(p):
                bc = []
                for (pz, r, zi, rr) in (
                        (psA, 0, 0, 0), (psA, 64, 0, 32),
                        (psB, 0, 0, 64), (psB, 64, 1, 0)):
                    bc.append(op("tensor",
                                 lambda t, pzz=pz, ro=r, z=zi, rw=rr, q=p["QW"]:
                                 t.matmul(
                                     pzz[ro:ro + 64, 1024:1024 + q],
                                     sel[rw][:, 0:64], rzsb[z][:, 0:q],
                                     start=True, stop=True),
                                 waits=(p["t_r1"],) if not bc else ()))
                t_bc = mark("tensor", "T")
                op("vec", lambda v, g_=p["g"], q0=p["Q0"], q=p["QW"]: v.tensor_tensor(
                    arnA[2 * g_][:, q0:q0 + q], atmp[:, 0:q],
                    psA[:, 1024:1024 + q], OP.mult), waits=(t_bc, p["t_cp"]))
                t_ev = op("vec", lambda v, g_=p["g"], q0=p["Q0"], q=p["QW"]:
                          v.tensor_tensor(
                              arnA[2 * g_ + 1][:, q0:q0 + q],
                              atmp[:, 512:512 + q],
                              psB[:, 1024:1024 + q], OP.mult), inc="V")
                return t_ev

            pend = None
            for gi, (Q0, QW, g, kts) in enumerate(groups):
                pv_started = [False] * 4
                z_started = [False] * 4
                t_pvz = None
                pv_gate = pend["t_cp"] if pend is not None else st["evacV"]
                for ki, kt in enumerate(kts):
                    par = st["ki"] % 2
                    st["ki"] += 1
                    psc = psA if par == 0 else psB
                    k0 = 128 * kt
                    qlo = max(k0 - 32, 32)
                    qhi = min(k0 + 160, QEND)
                    w0 = list(waits) + [st["exp"][par], t_consts_gd]
                    op("tensor", lambda t, p=psc: t.matmul(
                        p[:, 0:512], idn[:], mask4[:, 0:512], start=True, stop=True),
                        waits=tuple(w0))
                    op("tensor", lambda t, p=psc: t.matmul(
                        p[:, 512:768], idn[:], mask4[:, 512:768], start=True, stop=True))
                    def emit_score(j):
                        hh = 4 * g + j
                        ch = hh % 8
                        rb = 64 * (hh // 8)
                        cs = 192 * j + (qlo - (k0 - 32))
                        ce = cs + (qhi - qlo)
                        s0 = cs
                        while s0 < ce:
                            s1 = min(ce, ((s0 // 512) + 1) * 512)
                            sn = s1 - s0
                            op("tensor", lambda t, p=psc, c=ch, r=rb, k=k0, q0=qlo,
                               s=s0, m=sn, base=cs: t.matmul(
                                   p[:, s:s + m],
                                   arnK[c][r:r + 64, k:k + 128],
                                   arnQ[c][r:r + 64, q0 + (s - base):q0 + (s - base) + m],
                                   start=False, stop=True))
                            s0 = s1
                    for j in range(4):
                        emit_score(j)
                    t_sc = mark("tensor", "T")
                    if ki == 0 and pend is not None:
                        st["evacV"] = emit_tail(pend)
                        pend = None
                    st["exp"][par] = op("scalar", lambda s, p=psc, sl=par: s.activation(
                        ptb[:, 768 * sl:768 * sl + 768], p[:, 0:768], AF.Exp,
                        bias=0.0, scale=0.125), waits=(t_sc,), inc="S")
                    t_pv = emit_pv(g, kt, par, Q0, QW, qlo, qhi, k0,
                                   pv_started, st["exp"][par], pv_gate)
                    t_pvz = emit_z(g, kt, par, Q0, QW, qlo, qhi, k0,
                                   z_started, st["evacV"]) or t_pv
                t_cp0 = op("scalar", lambda s, q=QW: s.activation(
                    atmp[:, 0:q], psC[:, 0:q], AF.Copy, bias=0.0, scale=1.0),
                    waits=(t_pvz, st["evacV"]))
                t_cp = op("scalar", lambda s, q=QW: s.activation(
                    atmp[:, 512:512 + q], psC[:, 512:512 + q], AF.Copy,
                    bias=0.0, scale=1.0), inc="S")
                t_c0 = op("vec", lambda v, q=QW: v.tensor_scalar(
                    rztmp[:, 0:q], psA[:, 1024:1024 + q], 1e-30, None, OP.max),
                    waits=(t_pvz,), inc="V")
                op("scalar", lambda s, q=QW: s.activation(
                    rztmp[:, 0:q], rztmp[:, 0:q], AF.Ln,
                    bias=epsz[:, 0:1], scale=1.0), waits=(t_c0,))
                t_e0 = op("scalar", lambda s, q=QW: s.activation(
                    rzsb[0][:, 0:q], rztmp[:, 0:q], AF.Exp,
                    bias=0.0, scale=-1.0), inc="S")
                t_c1 = op("vec", lambda v, q=QW: v.tensor_scalar(
                    rztmp[:, 0:q], psB[:, 1024:1024 + q], 1e-30, None, OP.max),
                    waits=(t_e0,), inc="V")
                op("scalar", lambda s, q=QW: s.activation(
                    rztmp[:, 0:q], rztmp[:, 0:q], AF.Ln,
                    bias=epsz[:, 0:1], scale=1.0), waits=(t_c1,))
                t_r1 = op("scalar", lambda s, q=QW: s.activation(
                    rzsb[1][:, 0:q], rztmp[:, 0:q], AF.Exp,
                    bias=0.0, scale=-1.0), inc="S")
                pend = {"QW": QW, "g": g, "Q0": Q0, "t_r1": t_r1, "t_cp": t_cp}
            t_evac = emit_tail(pend)
            st["evacV"] = t_evac
            return t_evac

        def emit_pv(g, kt, par, Q0, QW, qlo, qhi, k0, pv_started, t_exp, pv_gate):
            clo = max(qlo, Q0)
            chi = min(qhi, Q0 + QW)
            if clo >= chi:
                return None
            cw = chi - clo
            first = True
            for j in range(4):
                hh = 4 * g + j
                bank = j // 2
                rb = 64 * (j % 2)
                ptoff = 768 * par + 192 * j + (qlo - (k0 - 32)) + (clo - qlo)
                stt = not pv_started[j]
                pv_started[j] = True
                op("tensor", lambda t, k=kt, h_=hh, b=bank, r=rb, po=ptoff, c0=clo,
                   m=cw, st_=stt: t.matmul(
                       psC[r:r + 64, 512 * b + (c0 - Q0):512 * b + (c0 - Q0) + m],
                       arnV[:, 1024 * k + 64 * h_:1024 * k + 64 * h_ + 64],
                       ptb[:, po:po + m], start=st_, stop=True),
                   waits=(t_exp, pv_gate) if first else ())
                first = False
            return mark("tensor", "T")

        def emit_z(g, kt, par, Q0, QW, qlo, qhi, k0, z_started, z_gate):
            clo = max(qlo, Q0)
            chi = min(qhi, Q0 + QW)
            if clo >= chi:
                return None
            cw = chi - clo
            zmap = ((psA, 0), (psA, 32), (psA, 64), (psB, 0))
            zfirst = True
            for j in range(4):
                pz, r = zmap[j]
                ptoff = 768 * par + 192 * j + (qlo - (k0 - 32)) + (clo - qlo)
                stt = not z_started[j]
                z_started[j] = True
                op("tensor", lambda t, p=pz, ro=r, po=ptoff, c0=clo, m=cw,
                   st_=stt: t.matmul(
                       p[ro:ro + 1, 1024 + (c0 - Q0):1024 + (c0 - Q0) + m],
                       onesb[:], ptb[:, po:po + m], start=st_, stop=True),
                   waits=(z_gate,) if (zfirst and stt) else ())
                zfirst = False
            return mark("tensor", "T")

        w_toks = {}
        cross = {"final_V": None, "out_SD": None,
                 "ffnout_V": None, "xbf_next": None}

        def dump(arena, tok):
            for cc in range(NCC):
                op("gdma", lambda g, c=cc: g.dma_start(
                    out=dbg[:, KVA * c:KVA * c + KVA], in_=arena[c][:, 0:KVA]),
                    waits=(tok,), inc="GD")

        def dump1(ap, cols, tok):
            op("gdma", lambda g: g.dma_start(out=dbg[:, 0:cols], in_=ap),
               waits=(tok,), inc="GD")

        def run_halves():
          for h in range(2):
            _CURH[0] = h
            XL = HTOK * h

            PH[0] = f"h{h}:xbf"
            if h == 0:
                t_xbf = None
                xbf_toks = {}
                for bi, (c0, n) in enumerate(BLK_S):
                    nn = KVA - c0 if bi == len(BLK_S) - 1 else n
                    for cc in range(NCC):
                        t_xbf = op("gdma", lambda g, c=cc, a=c0, m=nn, X=XL:
                                   g.dma_start(
                                       out=arnA[c][:, a:a + m],
                                       in_=xTs[128 * c:128 * c + 128,
                                               X + a:X + a + m]), inc="GD")
                        xbf_toks[(cc, bi)] = (t_xbf,)
            else:
                t_xbf = cross["xbf_next"]
                xbf_toks = cross["xbf_toks"]
            if h == 0:
                w_toks["wq"] = load_weight("wq", 0, ())
                w_toks["wk"] = load_weight("wk", 1, ())
            t_x2ld = []
            for cc in range(NCC):
                t_x2ld.append(op("sdma", lambda s, c=cc, X=XL: s.dma_start(
                    out=x2[c][:, 32:1057],
                    in_=xTs[128 * c:128 * c + 128, X + 32:X + 1057]),
                    waits=(cross["out_SD"],) if cc == 0 else (), inc="SD"))

            sA = lambda c, a, m: arnA[c][:, a:a + m]
            sS = lambda c, a, m: arnS[c][:, a:a + m]
            sQ = lambda c, a, m: arnQ[c][:, a:a + m]
            sK = lambda c, a, m: arnK[c][:, a:a + m]
            sX2 = lambda c, a, m: x2[c][:, a:a + m]

            PH[0] = f"h{h}:LN1"
            t_st1 = ln_stats(sA, BLK_S, 0, "1",
                             (t_xbf, t_ones, cross["final_V"], t_consts_gd), sq_fn=sS)
            t_xs1, ap1 = ln_apply2(sA, arnS, arnA, BLK_S, 0, "1", (t_st1,))
            q_toks = {}
            for cc in range(NCC):
                q_toks[(cc, 0)] = (ap1[(0, cc)], ap1[(1, cc)])
                q_toks[(cc, 1)] = (ap1[(1, cc)],)
                q_toks[(cc, 2)] = (ap1[(1, cc)], ap1[(2, cc)])
            k_toks = {(cc, b): (ap1[(b, cc)],) for cc in range(NCC) for b in range(3)}
            if h == DBG_H and DBG == "xs1":
                dump(arnA, t_xs1)
            _ckpt(1)

            PH[0] = f"h{h}:qkv"
            def ev_q(oc, bi, c0, n, pp, w):
                return op("scalar", lambda s, o=oc, a=c0, m=n, b=bi, p=pp: s.activation(
                    arnQ[o][:, a:a + m], p[:, 512 * b:512 * b + m], AF.Identity,
                    bias=qb_s[:, o:o + 1], scale=1.0),
                    waits=tuple(w) + ((cross["final_V"],) if (oc == 0 and bi == 0) else ()),
                    inc="S" if bi == 2 else None)

            t_qmm, t_qe = proj_cm(0, w_toks["wq"], sA, BLK_Q, ev_q, (),
                                  src_toks=q_toks)

            def ev_k(oc, bi, c0, n, pp, w):
                return op("scalar", lambda s, o=oc, a=c0, m=n, b=bi, p=pp: s.activation(
                    arnK[o][:, a:a + m], p[:, 512 * b:512 * b + m], AF.Copy,
                    bias=0.0, scale=1.0),
                    waits=tuple(w) + ((cross["ffnout_V"],) if (oc == 0 and bi == 0) else ()),
                    inc="S" if bi == 2 else None)

            w_toks["wv"] = load_weight("wv", 0, (t_qmm,))
            t_kmm, t_ke = proj_cm(1, w_toks["wk"], sA, BLK_S, ev_k, (),
                                  bias_fold=kbrow, XL=XL, src_toks=k_toks)
            w_toks["wo"] = load_weight("wo", 1, (t_kmm,))
            t_vmm, t_ve = proj_tm(w_toks["wv"], (), XL, src_toks=k_toks)
            w_toks["wr"] = load_weight("wr", 0, (t_vmm,))
            if h == DBG_H and DBG == "q":
                dump(arnQ, t_qe)
            if h == DBG_H and DBG == "k":
                dump(arnK, t_ke)
            if h == DBG_H and DBG == "v":
                for _kt in range(NKT):
                    op("gdma", lambda g, k=_kt: g.dma_start(
                        out=dbg[:, 1024 * k:1024 * k + 1024],
                        in_=arnV[:, 1024 * k:1024 * k + 1024]),
                        waits=(t_ve,), inc="GD")
            _ckpt(2)

            PH[0] = f"h{h}:attn"
            t_attnT = attention((t_qe, t_ke, t_ve))
            if h == DBG_H and DBG == "attnT":
                dump(arnA, t_attnT)
            _ckpt(3)

            PH[0] = f"h{h}:oproj"
            otoks = {}

            def ev_o(oc, bi, c0, n, pp, w):
                tok = op("scalar", lambda s, o=oc, a=c0, m=n, b=bi, p=pp: s.activation(
                    arnK[o][:, a:a + m], p[:, 512 * b:512 * b + m], AF.Identity,
                    bias=ob_s[:, o:o + 1], scale=1.0),
                    waits=w, inc="S" if bi == 2 else None)
                if bi == 2:
                    otoks[oc] = tok
                return tok

            t_omm, t_oe = proj_cm(1, w_toks["wo"], sA, BLK_Q, ev_o, (t_attnT,))
            w_toks["wk2"] = load_weight("wk2", 1, (t_omm,))
            if h == DBG_H and DBG == "attnout":
                dump(arnK, t_oe)
            _ckpt(4)

            PH[0] = f"h{h}:LNa"
            t_sta = ln_stats(sK, BLK_Q, 32, "a", (t_oe,), sq_fn=sS)
            if h == DBG_H and DBG == "rowsa":
                op("gdma", lambda g: g.dma_start(out=dbg[0:1, 0:KVA],
                                                 in_=rMS["a"][0:1, 0:KVA]),
                   waits=(t_sta,), inc="GD")
                op("gdma", lambda g: g.dma_start(out=dbg[1:2, 0:KVA],
                                                 in_=rA["a"][0:1, 0:KVA]),
                   waits=(t_sta,), inc="GD")
            pi = 0
            t_x2 = None
            x2toks = {}
            bctok = [None, None]
            for bi, (c0, n) in enumerate(BLK_Q):
                pb = psC if bi % 2 == 0 else psA
                o0 = c0 - 32
                w0 = [t_sta] + ([bctok[bi % 2]] if bctok[bi % 2] else [])
                t_bca = op("tensor", lambda t, p=pb, o=o0, m=n: t.matmul(
                    p[:, 0:m], ones1[:], rA["a"][0:1, o:o + m], start=True, stop=True),
                    waits=tuple(w0), inc="T")
                for cc in range(NCC):
                    op("vec", lambda v, c=cc, a=c0, m=n, sl=pi % 2, p=pb: v.tensor_tensor(
                        xstage[:, 512 * sl:512 * sl + m], arnK[c][:, a:a + m],
                        p[:, 0:m], OP.mult),
                        waits=(t_bca,) if cc == 0 else ())
                    t_x2 = op("vec", lambda v, c=cc, a=c0, m=n, sl=pi % 2: v.tensor_tensor(
                        x2[c][:, a:a + m], x2[c][:, a:a + m],
                        xstage[:, 512 * sl:512 * sl + m], OP.add),
                        waits=(t_x2ld[-1],), inc="V")
                    x2toks[(bi, cc)] = t_x2
                    pi += 1
                bctok[bi % 2] = t_x2

            if h == DBG_H and DBG == "x2":
                for cc in range(NCC):
                    op("gdma", lambda g, c=cc: g.dma_start(
                        out=dbg[:, KVA * c:KVA * c + KVA], in_=x2[c][:, 0:KVA]),
                        waits=(t_x2,), inc="GD")
            _ckpt(5)
            PH[0] = f"h{h}:LN2"
            cast_toks = {}
            for cc in range(NCC):
                for bi, (c0, n) in enumerate(BLK_Q):
                    cast_toks[(cc, bi)] = (op(
                        "scalar", lambda s, c=cc, a=c0, m=n: s.activation(
                            arnQ[c][:, a:a + m], x2[c][:, a:a + m], AF.Copy,
                            bias=0.0, scale=1.0), waits=(x2toks[(bi, cc)],),
                        inc="S"),)
            t_st2 = ln_stats(sQ, BLK_Q, 32, "2", (t_x2,), sq_fn=sS,
                             src_toks=cast_toks)
            t_a2m = op("vec", lambda v, X=XL: v.tensor_tensor(
                rA["2"][0:1, 0:1025], rA["2"][0:1, 0:1025],
                kvm[0:1, X + 32:X + 32 + 1025], OP.mult), waits=(t_st2,), inc="V")
            t_ln2, _ap2 = ln_apply2(sX2, arnQ, arnQ, BLK_Q, 32, "2", (t_a2m,))

            if h == DBG_H and DBG == "ln2":
                dump(arnQ, t_ln2)
            _ckpt(6)
            PH[0] = f"h{h}:mix"
            t_mix = None
            mix_toks = {}
            for cc in range(NCC):
                for (c0, n) in BLK_O:
                    op("vec", lambda v, c=cc, a=c0, m=n: v.tensor_tensor(
                        arnK[c][:, a:a + m], arnQ[c][:, a:a + m],
                        arnQ[c][:, a - 1:a - 1 + m], OP.subtract),
                        waits=(t_ln2,) if cc == 0 else ())
                for (c0, n) in BLK_O:
                    op("vec", lambda v, c=cc, a=c0, m=n: v.scalar_tensor_tensor(
                        arnV[:, 1152 * c + a:1152 * c + a + m], arnK[c][:, a:a + m],
                        mur_s[:, c:c + 1], arnQ[c][:, a - 1:a - 1 + m],
                        OP.mult, OP.add), waits=())
                for (c0, n) in BLK_O:
                    t_mix = op("vec", lambda v, c=cc, a=c0, m=n: v.scalar_tensor_tensor(
                        arnS[c][:, a:a + m], arnK[c][:, a:a + m],
                        muk_s[:, c:c + 1], arnQ[c][:, a - 1:a - 1 + m],
                        OP.mult, OP.add), waits=(),
                        inc="V" if c0 == 545 else None)
                for b in range(2):
                    mix_toks[(cc, b)] = (t_mix,)

            if h == DBG_H and DBG == "xr":
                dump1(arnV[:, 0:8 * KVA], 8 * KVA, t_mix)
            if h == DBG_H and DBG == "xk":
                dump(arnS, t_mix)
            _ckpt(7)
            PH[0] = f"h{h}:ffn"
            sXR = lambda c, a, m: arnV[:, 1152 * c + a:1152 * c + a + m]

            def ev_r(oc, bi, c0, n, pp, w):
                return op("scalar", lambda s, o=oc, a=c0, m=n, b=bi, p=pp: s.activation(
                    arnK[o][:, a:a + m], p[:, 512 * b:512 * b + m], AF.Sigmoid,
                    bias=0.0, scale=1.0),
                    waits=tuple(w) + ((t_mix,) if (oc == 0 and bi == 0) else ()),
                    inc="S" if bi == len(BLK_O) - 1 else None)

            t_rmm, t_re = proj_cm(0, w_toks["wr"], sXR, BLK_O, ev_r, (),
                                  src_toks=mix_toks)
            w_toks["wv2"] = load_weight("wv2", 0, (t_rmm,))
            if h == 0:
                pre = None
                pre_toks = {}
                for cc in range(NCC):
                    pre = op("gdma", lambda g, c=cc: g.dma_start(
                        out=arnA[c][:, 0:KVA],
                        in_=xTs[128 * c:128 * c + 128, HTOK:HTOK + KVA]),
                        waits=(t_omm,) if cc == 0 else (), inc="GD")
                    for b in range(3):
                        pre_toks[(cc, b)] = (pre,)
                cross["xbf_next"] = pre
                cross["xbf_toks"] = pre_toks

            def ev_kr(oc, bi, c0, n, pp, w):
                return op("scalar", lambda s, o=oc, a=c0, m=n, b=bi, p=pp: s.activation(
                    arnV[:, 1152 * o + a:1152 * o + a + m],
                    p[:, 512 * b:512 * b + m], AF.Relu,
                    bias=0.0, scale=1.0), waits=w,
                    inc="S" if bi == len(BLK_O) - 1 else None)

            t_k2mm, t_k2e = proj_cm(1, w_toks["wk2"], sS, BLK_O, ev_kr, (),
                                    src_toks=mix_toks)
            if h == 0:
                w_toks["wk_next"] = load_weight("wk", 1, (t_k2mm,))
            t_sqk = None
            for cc in range(NCC):
                for (c0, n) in BLK_O:
                    t_sqk = op("vec", lambda v, c=cc, a=c0, m=n: v.tensor_tensor(
                        arnV[:, 1152 * c + a:1152 * c + a + m],
                        arnV[:, 1152 * c + a:1152 * c + a + m],
                        arnV[:, 1152 * c + a:1152 * c + a + m],
                        OP.mult), waits=(t_k2e,),
                        inc="V" if (cc == NCC - 1 and c0 == 545) else None)

            ftoks = {}

            def ev_f(oc, bi, c0, n, pp, w):
                tok = op("vec", lambda v, o=oc, a=c0, m=n, b=bi, p=pp: v.tensor_tensor(
                    arnQ[o][:, a:a + m], p[:, 512 * b:512 * b + m], arnK[o][:, a:a + m],
                    OP.mult), waits=tuple(w) + ((t_re,) if (oc == 0 and bi == 0) else ()),
                    inc="V" if bi == len(BLK_O) - 1 else None)
                if bi == len(BLK_O) - 1:
                    ftoks[oc] = tok
                return tok

            t_v2mm, t_fe = proj_cm(0, w_toks["wv2"], sXR, BLK_O, ev_f, (t_sqk,))
            cross["ffnout_V"] = t_fe
            if h == 0:
                w_toks["wq_next"] = load_weight("wq", 0, (t_v2mm,))

            PH[0] = f"h{h}:LNfout"
            if h == DBG_H and DBG == "r":
                dump(arnK, t_re)
            if h == DBG_H and DBG == "kact":
                dump(arnA, t_sqk)
            if h == DBG_H and DBG == "ffnout":
                dump(arnQ, t_fe)
            _ckpt(8)
            t_stf = ln_stats(sQ, BLK_O, 33, "f", (t_fe,), sq_fn=sS, want_ms=False)
            op("vec", lambda v: v.tensor_tensor(
                rT1[0:1, 0:1024], rMS["a"][0:1, 1:1025], rSQ[0:1, 0:1024], OP.mult),
                waits=(t_stf,))
            t_d = op("vec", lambda v: v.tensor_tensor(
                rD[0:1, 0:1024], rT1[0:1, 0:1024], psA[0:1, 0:1024], OP.add), inc="V")

            if h == DBG_H and DBG == "rows":
                op("gdma", lambda g: g.dma_start(out=dbg[0:1, 0:KVA],
                                                 in_=rA["f"][0:1, 0:KVA]),
                   waits=(t_d,), inc="GD")
                op("gdma", lambda g: g.dma_start(out=dbg[1:2, 0:KVA],
                                                 in_=rD[0:1, 0:KVA]),
                   waits=(t_d,), inc="GD")
                op("gdma", lambda g: g.dma_start(out=dbg[2:3, 0:KVA],
                                                 in_=rMS["a"][0:1, 0:KVA]),
                   waits=(t_d,), inc="GD")
            t_out = None
            bctok2 = [None, None]
            t_fin = None
            for bi, (c0, n) in enumerate(BLK_O):
                pb = psC if bi % 2 == 0 else psA
                o0 = c0 - 33
                w0 = [t_d] + ([bctok2[bi % 2]] if bctok2[bi % 2] else [])
                op("tensor", lambda t, p=pb, o=o0, m=n: t.matmul(
                    p[:, 0:m], onesdiv[:], rD[0:1, o:o + m], start=True, stop=True),
                    waits=tuple(w0))
                t_bcf = op("tensor", lambda t, p=pb, o=o0, m=n: t.matmul(
                    p[:, 512:512 + m], ones1[:], rA["f"][0:1, o:o + m],
                    start=True, stop=True), inc="T")
                for cc in range(NCC):
                    op("vec", lambda v, c=cc, a=c0, m=n, p=pb: v.tensor_tensor(
                        bstage[:, 0:m], arnQ[c][:, a:a + m], p[:, 0:m], OP.subtract),
                        waits=(t_bcf,) if cc == 0 else ())
                    op("vec", lambda v, c=cc, a=c0, m=n, p=pb: v.tensor_tensor(
                        xstage[:, 0:m], bstage[:, 0:m], p[:, 512:512 + m], OP.mult),
                        waits=())
                    t_fin = op("vec", lambda v, c=cc, a=c0, m=n: v.tensor_tensor(
                        x2[c][:, a:a + m], xstage[:, 0:m], x2[c][:, a:a + m], OP.add),
                        inc="V")
                    t_out = op("sdma", lambda s, c=cc, a=c0, m=n, hh=h: s.dma_start(
                        out=yT[128 * c:128 * c + 128,
                               HTOK * hh + a - 33:HTOK * hh + a - 33 + m],
                        in_=x2[c][:, a:a + m]), waits=(t_fin,), inc="SD")
                bctok2[bi % 2] = t_fin
            cross["final_V"] = t_fin
            cross["out_SD"] = t_out
            if h == 0:
                w_toks["wq"] = w_toks.pop("wq_next")
                w_toks["wk"] = w_toks.pop("wk_next")
                _ckpt(9)

        try:
            run_halves()
        except _StopBuild:
            pass

        waited = {}

        def emit_stream(eng_obj, eng_name):
            for (e, builder, ws, inc, amt, _phl) in OPS:
                if e != eng_name:
                    continue
                for (sname, val) in ws:
                    key = (eng_name, sname)
                    if waited.get(key, -1) >= val:
                        continue
                    waited[key] = val
                    eng_obj.wait_ge(sems[sname], val)
                ins = builder(eng_obj)
                if inc is not None:
                    ins.then_inc(sems[inc], amt)

        @block.tensor
        def _(t):
            emit_stream(t, "tensor")

        @block.vector
        def _(v):
            emit_stream(v, "vec")

        @block.scalar
        def _(s):
            emit_stream(s, "scalar")

        @block.gpsimd
        def _(g):
            emit_stream(g, "gdma")
            g.wait_ge(sems["GD"], CNT["GD"])

        @block.sync
        def _(s):
            emit_stream(s, "sdma")
            s.wait_ge(sems["SD"], CNT["SD"])

        _BUILD_LOG["ops"] = [(e, tuple(ws), inc, phl)
                             for (e, b, ws, inc, amt, phl) in OPS]
    return nc


def _host_prep(inputs):
    x = np.asarray(inputs["L0_attn_hidden_states"], dtype=np.float32).reshape(SQ, CM)
    xT = np.ascontiguousarray(x.T)
    XP = np.zeros((CM, 33 + SQ + (XCOLS - 33 - TOK)), dtype=np.float32)
    XP = np.zeros((CM, SQ + XCOLS), dtype=np.float32)
    XP[:, 33:33 + SQ] = xT
    kvm_full = np.zeros((1, SQ + XCOLS), dtype=np.float32)
    kvm_full[0, 33:33 + SQ] = 1.0

    def wt(n):
        return np.ascontiguousarray(np.asarray(inputs[n], dtype=np.float32).T)

    perm = np.concatenate([np.r_[64 * c:64 * c + 64, 64 * (c + 8):64 * (c + 8) + 64]
                           for c in range(8)])
    w = {
        "wq": np.ascontiguousarray(wt("L0_attn_q_w")[:, perm]),
        "wk": np.ascontiguousarray(wt("L0_attn_k_w")[:, perm]),
        "wv": wt("L0_attn_v_w"),
        "wo": wt("L0_attn_o_w"), "wr": wt("L0_ffn_W_r"), "wk2": wt("L0_ffn_W_k"),
        "wv2": wt("L0_ffn_W_v"),
    }

    def col8(n):
        return np.ascontiguousarray(
            np.asarray(inputs[n], dtype=np.float32).reshape(8, 128).T)

    qb8 = np.ascontiguousarray(
        np.asarray(inputs["L0_attn_q_b"], dtype=np.float32)[perm].reshape(8, 128).T)
    ob8 = col8("L0_attn_o_b")
    mur8 = col8("L0_ffn_mu_r")
    muk8 = col8("L0_ffn_mu_k")
    kbrow = np.asarray(inputs["L0_attn_k_b"], dtype=np.float32)[perm].reshape(1, CM)
    vbrow = np.asarray(inputs["L0_attn_v_b"], dtype=np.float32).reshape(1, CM)

    kk = np.arange(128)[:, None]
    qq = np.arange(192)[None, :]
    band = ((qq - kk >= 0) & (qq - kk <= 64))
    m1 = np.where(band, 0.0, MASKVAL).astype(np.float32)
    mask4 = np.ascontiguousarray(np.tile(m1, (1, 4)))
    idn = np.eye(128, dtype=np.float32)

    in_maps = []
    for c in range(8):
        s = TOK * c
        im = {
            "xTs": np.ascontiguousarray(XP[:, s:s + XCOLS]),
            "kvm": np.ascontiguousarray(kvm_full[:, s:s + XCOLS]),
            "qb8": qb8, "ob8": ob8, "mur8": mur8, "muk8": muk8,
            "kbrow": kbrow, "vbrow": vbrow, "mask4": mask4, "idn": idn,
        }
        im.update(w)
        in_maps.append(im)
    return in_maps


def kernel(**inputs):
    in_maps = _host_prep(inputs)
    if "nc" not in _NC_CACHE:
        _NC_CACHE["nc"] = build_program()
    nc = _NC_CACHE["nc"]
    res = run_bass_kernel_spmd(nc, in_maps, core_ids=list(range(8)))
    yT_full = np.concatenate([np.asarray(r["yT"], dtype=np.float32)
                              for r in res.results], axis=1)
    out = np.ascontiguousarray(yT_full.T).reshape(1, SQ, CM).astype(np.float32)
    return out


# version 30
# speedup vs baseline: 1.0374x; 1.0158x over previous
import numpy as np
import concourse.bass as bass
import concourse.mybir as mybir
from concourse.bass_utils import run_bass_kernel_spmd
from concourse.alu_op_type import AluOpType as OP
from contextlib import ExitStack

F32 = mybir.dt.float32
BF16 = mybir.dt.bfloat16
AF = mybir.ActivationFunctionType

SQ = 16384
CM = 1024
NCC = 8
H = 16
HD = 64
TOK = 2048
HTOK = 1024
KVA = 1152
XCOLS = 2176
NKT = 9
MASKVAL = -1e30

BLK_S = [(0, 512), (512, 512), (1024, 128)]
BLK_Q = [(32, 512), (544, 257), (801, 256)]
BLK_O = [(33, 512), (545, 512)]
QEND = 1057

WNAMES = ["wq", "wk", "wv", "wo", "wr", "wk2", "wv2"]

_NC_CACHE = {}
_BUILD_LOG = {}
PH = [""]
STOP_PHASE = 99
DISABLE = set()
DBG = None
DBG_H = 0
DBG_G = 0


class _StopBuild(Exception):
    pass


_CURH = [0]


def _ckpt(n):
    if STOP_PHASE <= n + 10 * _CURH[0]:
        raise _StopBuild()


def build_program():
    nc = bass.Bass("TRN2", target_bir_lowering=False)

    xTs = nc.dram_tensor("xTs", [CM, XCOLS], F32, kind="ExternalInput").ap()
    wd = {n: nc.dram_tensor(n, [CM, CM], F32, kind="ExternalInput").ap() for n in WNAMES}
    qb8_d = nc.dram_tensor("qb8", [128, 8], F32, kind="ExternalInput").ap()
    ob8_d = nc.dram_tensor("ob8", [128, 8], F32, kind="ExternalInput").ap()
    mur8_d = nc.dram_tensor("mur8", [128, 8], F32, kind="ExternalInput").ap()
    muk8_d = nc.dram_tensor("muk8", [128, 8], F32, kind="ExternalInput").ap()
    kbrow_d = nc.dram_tensor("kbrow", [1, CM], F32, kind="ExternalInput").ap()
    vbrow_d = nc.dram_tensor("vbrow", [1, CM], F32, kind="ExternalInput").ap()
    kvm_d = nc.dram_tensor("kvm", [1, XCOLS], F32, kind="ExternalInput").ap()
    mask_d = nc.dram_tensor("mask4", [128, 768], F32, kind="ExternalInput").ap()
    idn_d = nc.dram_tensor("idn", [128, 128], F32, kind="ExternalInput").ap()
    yT = nc.dram_tensor("yT", [CM, TOK], F32, kind="ExternalOutput").ap()
    dbg = nc.dram_tensor("dbg", [128, NKT * 1024], BF16, kind="ExternalOutput").ap()

    OPS = []
    CNT = {k: 0 for k in ("T", "V", "S", "GD", "SD")}
    TOKENG = {}

    def op(eng, builder, waits=(), inc=None, amt=None):
        tok = None
        if inc is not None:
            if amt is None:
                amt = 16 if inc in ("GD", "SD") else 1
            CNT[inc] += amt
            tok = (inc, CNT[inc])
            TOKENG[tok] = eng
        ws = []
        for t in waits:
            if t is None:
                continue
            if TOKENG.get(t) == eng and t[0] not in ("GD", "SD"):
                continue
            ws.append(t)
        OPS.append((eng, builder, ws, inc, amt, PH[0]))
        return tok

    def mark(eng, inc, amt=None):
        if amt is None:
            amt = 16 if inc in ("GD", "SD") else 1
        for i in range(len(OPS) - 1, -1, -1):
            if OPS[i][0] == eng:
                e, b, ws, oldinc, oldamt, phl = OPS[i]
                assert oldinc is None, "op already carries an inc"
                CNT[inc] += amt
                tok = (inc, CNT[inc])
                TOKENG[tok] = eng
                OPS[i] = (e, b, ws, inc, amt, phl)
                return tok
        raise RuntimeError("no op to mark")

    with ExitStack() as stk:
        en = stk.enter_context
        arnA = [en(nc.sbuf_tensor(f"arnA{i}", [128, KVA], BF16)) for i in range(NCC)]
        arnS = [en(nc.sbuf_tensor(f"arnS{i}", [128, KVA], BF16)) for i in range(NCC)]
        arnQ = [en(nc.sbuf_tensor(f"arnQ{i}", [128, KVA], BF16)) for i in range(NCC)]
        arnK = [en(nc.sbuf_tensor(f"arnK{i}", [128, KVA], BF16)) for i in range(NCC)]
        arnV = en(nc.sbuf_tensor("arnV", [128, NKT * 1024], BF16))
        x2 = [en(nc.sbuf_tensor(f"x2_{i}", [128, KVA], F32)) for i in range(NCC)]
        ptb = en(nc.sbuf_tensor("ptb", [128, 2 * 768], BF16))
        wbuf = [en(nc.sbuf_tensor(f"wbuf{i}", [128, NCC * 1024], BF16)) for i in range(2)]
        rzsb = [en(nc.sbuf_tensor(f"rzsb{i}", [128, 512], BF16)) for i in range(2)]
        rztmp = en(nc.sbuf_tensor("rztmp", [128, 512], F32))
        xstage = en(nc.sbuf_tensor("xstage", [128, 1024], F32))
        bstage = en(nc.sbuf_tensor("bstage", [128, 512], BF16))
        atmp = en(nc.sbuf_tensor("atmp", [128, 1024], BF16))
        rT1 = en(nc.sbuf_tensor("rT1", [1, KVA], F32))
        rSQ = en(nc.sbuf_tensor("rSQ", [1, KVA], F32))
        rA = {k: en(nc.sbuf_tensor(f"rA_{k}", [1, KVA], BF16)) for k in ("1", "a", "f")}
        rA["2"] = rA["1"]
        rMS = {k: en(nc.sbuf_tensor(f"rMS_{k}", [1, KVA], BF16)) for k in ("1", "a")}
        rMS["2"] = rMS["1"]
        rD = en(nc.sbuf_tensor("rD", [1, KVA], BF16))
        onesb = en(nc.sbuf_tensor("onesb", [128, 1], BF16))
        epsb = en(nc.sbuf_tensor("epsb", [1, 1], F32))
        epsz = en(nc.sbuf_tensor("epsz", [128, 1], F32))
        sel = {r: en(nc.sbuf_tensor(f"sel{r}", [128, 64], BF16)) for r in (0, 32, 64)}
        ones1 = en(nc.sbuf_tensor("ones1", [1, 128], BF16))
        onesdiv = en(nc.sbuf_tensor("onesdiv", [1, 128], BF16))
        qb_s = en(nc.sbuf_tensor("qb_s", [128, 8], F32))
        ob_s = en(nc.sbuf_tensor("ob_s", [128, 8], F32))
        mur_s = en(nc.sbuf_tensor("mur_s", [128, 8], F32))
        muk_s = en(nc.sbuf_tensor("muk_s", [128, 8], F32))
        kbrow = en(nc.sbuf_tensor("kbrow_s", [1, CM], BF16))
        vbrow = en(nc.sbuf_tensor("vbrow_s", [1, CM], BF16))
        kvm = en(nc.sbuf_tensor("kvm_s", [1, XCOLS], BF16))
        mask4 = en(nc.sbuf_tensor("mask4_s", [128, 768], BF16))
        idn = en(nc.sbuf_tensor("idn_s", [128, 128], BF16))
        psA = en(nc.psum_tensor("psA", [128, 1536], F32))
        psB = en(nc.psum_tensor("psB", [128, 1536], F32))
        psC = en(nc.psum_tensor("psC", [128, 1024], F32))
        sems = {k: en(nc.semaphore(f"sem{k}")) for k in ("T", "V", "S", "GD", "SD")}
        en(nc.allow_low_precision(
            "bf16 row normalizers / bf16 matmul operands by design; all "
            "accumulation stays fp32 in PSUM"))
        block = en(nc.Block())

        for dst, src in ((qb_s, qb8_d), (ob_s, ob8_d), (mur_s, mur8_d), (muk_s, muk8_d)):
            op("sdma", lambda s, d=dst, so=src: s.dma_start(out=d[:], in_=so), inc="SD")
        gl = []
        for dst, src in ((kbrow, kbrow_d), (vbrow, vbrow_d), (kvm, kvm_d),
                         (mask4, mask_d), (idn, idn_d)):
            gl.append(op("gdma", lambda g, d=dst, so=src: g.dma_start(out=d[:], in_=so),
                         inc="GD"))
        t_consts_gd = gl[-1]
        op("vec", lambda v: v.memset(onesb[:], 1.0))
        op("vec", lambda v: v.memset(epsb[:], 1e-5))
        op("vec", lambda v: v.memset(epsz[:], 1e-30))
        op("vec", lambda v: v.memset(ones1[:], 1.0))
        for r in (0, 32, 64):
            op("vec", lambda v, rr=r: v.memset(sel[rr][:, 0:64], 0.0))
            op("vec", lambda v, rr=r: v.memset(sel[rr][rr:rr + 1, 0:64], 1.0))
        op("vec", lambda v: v.memset(atmp[:, 0:512], 1.0))
        t_ones = op("vec", lambda v: v.memset(onesdiv[:], 1.0 / 1024.0), inc="V")
        op("tensor", lambda t: t.matmul(psA[:, 1024:1536], idn[:], atmp[:, 0:512],
                                        start=True, stop=True),
           waits=(t_ones, t_consts_gd))
        op("tensor", lambda t: t.matmul(psB[:, 1024:1536], idn[:], atmp[:, 0:512],
                                        start=True, stop=True))

        def load_weight(name, bufi, waits):
            last = None
            for cc in range(NCC):
                last = op("gdma",
                          lambda g, b=wbuf[bufi], c=cc, n=name: g.dma_start(
                              out=b[:, 1024 * c:1024 * c + 1024],
                              in_=wd[n][128 * c:128 * c + 128, :]),
                          waits=waits if cc == 0 else (), inc="GD")
            return last

        def ln_stats(src_fn, blocks, shift, key, waits, sq_fn, want_ms=True,
                     src_toks=None):
            first = True
            last = None
            for bi, (c0, n) in enumerate(blocks):
                o0 = c0 - shift
                for cc in range(NCC):
                    ws = waits if first else ()
                    if src_toks is not None:
                        ws = tuple(ws) + tuple(src_toks.get((cc, bi), ()))
                    last = op("tensor",
                              lambda t, c=cc, a=c0, m=n, o=o0, st=(cc == 0): t.matmul(
                                  psA[0:1, o:o + m], onesb[:], src_fn(c, a, m),
                                  start=st, stop=True),
                              waits=ws)
                    first = False
            t_smm = mark("tensor", "T")
            last = None
            for cc in range(NCC):
                for bi, (c0, n) in enumerate(blocks):
                    ws = waits
                    if src_toks is not None:
                        ws = tuple(ws) + tuple(src_toks.get((cc, bi), ()))
                    last = op("vec", lambda v, c=cc, a=c0, m=n: v.tensor_tensor(
                        sq_fn(c, a, m), src_fn(c, a, m), src_fn(c, a, m), OP.mult),
                        waits=ws)
            t_sq = op("vec", lambda v: v.memset(rT1[0:1, KVA - 1:KVA], 0.0), inc="V")
            first = True
            for (c0, n) in blocks:
                o0 = c0 - shift
                for cc in range(NCC):
                    op("tensor", lambda t, c=cc, a=c0, m=n, o=o0, st=(cc == 0): t.matmul(
                        psB[0:1, o:o + m], onesb[:], sq_fn(c, a, m), start=st, stop=True),
                        waits=(t_sq,) if first else ())
                    first = False
            t_qmm = mark("tensor", "T")
            W = blocks[-1][0] + blocks[-1][1] - shift
            t_s2 = op("scalar", lambda s: s.activation(
                rT1[0:1, 0:W], psA[0:1, 0:W], AF.Square, bias=0.0, scale=1.0),
                waits=(t_smm,), inc="S")
            t_u = op("vec", lambda v: v.scalar_tensor_tensor(
                rT1[0:1, 0:W], psB[0:1, 0:W], 1024.0, rT1[0:1, 0:W],
                OP.mult, OP.subtract), waits=(t_s2, t_qmm), inc="V")
            op("scalar", lambda s: s.activation(
                rT1[0:1, 0:W], rT1[0:1, 0:W], AF.Ln, bias=epsb[0:1, 0:1],
                scale=2.0 ** -20), waits=(t_u,))
            t_a = op("scalar", lambda s: s.activation(
                rA[key][0:1, 0:W], rT1[0:1, 0:W], AF.Exp, bias=0.0,
                scale=-0.5), inc="S")
            if want_ms:
                return op("vec", lambda v: v.tensor_tensor(
                    rMS[key][0:1, 0:W], psA[0:1, 0:W], rA[key][0:1, 0:W], OP.mult),
                    waits=(t_a,), inc="V")
            return op("scalar", lambda s: s.activation(
                rSQ[0:1, 0:W], rT1[0:1, 0:W], AF.Exp, bias=0.0,
                scale=0.5), inc="S")

        def ln_apply2(src_fn, mid, dst, blocks, shift, key, waits):
            prevtok = [None, None]
            last = None
            tokmap = {}
            for bi, (c0, n) in enumerate(blocks):
                o0 = c0 - shift
                pb = psC if bi % 2 == 0 else psA
                w0 = list(waits) + ([prevtok[bi % 2]] if prevtok[bi % 2] else [])
                op("tensor", lambda t, p=pb, o=o0, m=n: t.matmul(
                    p[:, 0:m], onesdiv[:], rMS[key][0:1, o:o + m], start=True, stop=True),
                    waits=tuple(w0))
                tb = op("tensor", lambda t, p=pb, o=o0, m=n: t.matmul(
                    p[:, 512:512 + m], ones1[:], rA[key][0:1, o:o + m],
                    start=True, stop=True), inc="T")
                for cc in range(NCC):
                    op("vec", lambda v, c=cc, a=c0, m=n, p=pb: v.tensor_tensor(
                        mid[c][:, a:a + m], src_fn(c, a, m), p[:, 0:m], OP.subtract),
                        waits=(tb,) if cc == 0 else ())
                for cc in range(NCC):
                    tokmap[(bi, cc)] = op(
                        "vec", lambda v, c=cc, a=c0, m=n, p=pb: v.tensor_tensor(
                            dst[c][:, a:a + m], mid[c][:, a:a + m], p[:, 512:512 + m],
                            OP.mult), waits=(), inc="V")
                last = op("vec", lambda v: v.memset(rT1[0:1, KVA - 1:KVA], 0.0), inc="V")
                prevtok[bi % 2] = last
            return last, tokmap

        def proj_cm(bufi, w_tok, src_fn, blocks, evac, waits, bias_fold=None, XL=0,
                    src_toks=None):
            evac_tok = [None, None]
            t_mm_last = None
            last_ev = None
            for oc in range(NCC):
                pp = psA if oc % 2 == 0 else psB
                w0 = [w_tok] + list(waits) + ([evac_tok[oc % 2]] if evac_tok[oc % 2] else [])
                first = True
                for bi, (c0, n) in enumerate(blocks):
                    for cc in range(NCC):
                        ws = tuple(w0) if first else ()
                        if src_toks is not None:
                            ws = ws + tuple(src_toks.get((cc, bi), ()))
                        op("tensor", lambda t, p=pp, c=cc, o=oc, a=c0, m=n, b=bi,
                           st=(cc == 0), bf=bufi: t.matmul(
                               p[:, 512 * b:512 * b + m],
                               wbuf[bf][:, 1024 * c + 128 * o:1024 * c + 128 * o + 128],
                               src_fn(c, a, m), start=st, stop=True),
                           waits=ws)
                        first = False
                if bias_fold is not None:
                    brow = bias_fold
                    for bi, (c0, n) in enumerate(blocks):
                        op("tensor", lambda t, p=pp, o=oc, a=c0, m=n, b=bi: t.matmul(
                            p[:, 512 * b:512 * b + m],
                            brow[0:1, 128 * o:128 * o + 128],
                            kvm[0:1, XL + a:XL + a + m], start=False, stop=True))
                t_mm_last = mark("tensor", "T")
                for bi, (c0, n) in enumerate(blocks):
                    last_ev = evac(oc, bi, c0, n, pp, (t_mm_last,) if bi == 0 else ())
                evac_tok[oc % 2] = last_ev
            return t_mm_last, last_ev

        def proj_tm(w_tok, waits, XL, src_toks=None):
            evac_tok = [None, None]
            t_mm = None
            last_ev = None
            for tt in range(NKT):
                pp = psA if tt % 2 == 0 else psB
                w0 = [w_tok] + list(waits) + ([evac_tok[tt % 2]] if evac_tok[tt % 2] else [])
                first = True
                bt = min(tt // 4, 2)
                for cc in range(NCC):
                    for dv in range(2):
                        ws = tuple(w0) if first else ()
                        if src_toks is not None and dv == 0:
                            ws = ws + tuple(src_toks.get((cc, bt), ()))
                        op("tensor", lambda t, p=pp, c=cc, k=tt, d=dv, st=(cc == 0):
                           t.matmul(p[:, 512 * d:512 * d + 512],
                                    arnA[c][:, 128 * k:128 * k + 128],
                                    wbuf[0][:, 1024 * c + 512 * d:1024 * c + 512 * d + 512],
                                    start=st, stop=True),
                           waits=ws)
                        first = False
                for dv in range(2):
                    op("tensor", lambda t, p=pp, k=tt, d=dv: t.matmul(
                        p[:, 512 * d:512 * d + 512],
                        kvm[0:1, XL + 128 * k:XL + 128 * k + 128],
                        vbrow[0:1, 512 * d:512 * d + 512], start=False, stop=True))
                t_mm = mark("tensor", "T")
                last_ev = op("scalar", lambda s, p=pp, k=tt: s.activation(
                    arnV[:, 1024 * k:1024 * k + 1024], p[:, 0:1024], AF.Copy,
                    bias=0.0, scale=1.0), waits=(t_mm,), inc="S")
                evac_tok[tt % 2] = last_ev
            return t_mm, last_ev

        att_state = {"exp": [None, None], "msk": [None, None], "ki": 0,
                     "evacV": None, "evacT": None}

        def attention(waits):
            st = att_state
            groups = []
            for (Q0, QW) in BLK_Q:
                kts = [kt for kt in range(NKT)
                       if max(128 * kt - 32, 32) < Q0 + QW
                       and min(128 * kt + 160, QEND) > Q0]
                for g in range(4):
                    groups.append((Q0, QW, g, kts))

            def emit_tail(p):
                bc = []
                for (pz, r, zi, rr) in (
                        (psA, 0, 0, 0), (psA, 64, 0, 32),
                        (psB, 0, 0, 64), (psB, 64, 1, 0)):
                    bc.append(op("tensor",
                                 lambda t, pzz=pz, ro=r, z=zi, rw=rr, q=p["QW"]:
                                 t.matmul(
                                     pzz[ro:ro + 64, 1024:1024 + q],
                                     sel[rw][:, 0:64], rzsb[z][:, 0:q],
                                     start=True, stop=True),
                                 waits=(p["t_r1"],) if not bc else ()))
                t_bc = mark("tensor", "T")
                op("vec", lambda v, g_=p["g"], q0=p["Q0"], q=p["QW"]: v.tensor_tensor(
                    arnA[2 * g_][:, q0:q0 + q], atmp[:, 0:q],
                    psA[:, 1024:1024 + q], OP.mult), waits=(t_bc, p["t_cp"]))
                t_ev = op("vec", lambda v, g_=p["g"], q0=p["Q0"], q=p["QW"]:
                          v.tensor_tensor(
                              arnA[2 * g_ + 1][:, q0:q0 + q],
                              atmp[:, 512:512 + q],
                              psB[:, 1024:1024 + q], OP.mult), inc="V")
                return t_ev

            pend = None
            for gi, (Q0, QW, g, kts) in enumerate(groups):
                pv_started = [False] * 4
                z_started = [False] * 4
                t_pvz = None
                pv_gate = pend["t_cp"] if pend is not None else st["evacV"]
                for ki, kt in enumerate(kts):
                    par = st["ki"] % 2
                    st["ki"] += 1
                    psc = psA if par == 0 else psB
                    k0 = 128 * kt
                    qlo = max(k0 - 32, 32)
                    qhi = min(k0 + 160, QEND)
                    w0 = list(waits) + [st["exp"][par], t_consts_gd]
                    op("tensor", lambda t, p=psc: t.matmul(
                        p[:, 0:512], idn[:], mask4[:, 0:512], start=True, stop=True),
                        waits=tuple(w0))
                    op("tensor", lambda t, p=psc: t.matmul(
                        p[:, 512:768], idn[:], mask4[:, 512:768], start=True, stop=True))
                    def emit_score(j):
                        hh = 4 * g + j
                        ch = hh % 8
                        rb = 64 * (hh // 8)
                        cs = 192 * j + (qlo - (k0 - 32))
                        ce = cs + (qhi - qlo)
                        s0 = cs
                        while s0 < ce:
                            s1 = min(ce, ((s0 // 512) + 1) * 512)
                            sn = s1 - s0
                            op("tensor", lambda t, p=psc, c=ch, r=rb, k=k0, q0=qlo,
                               s=s0, m=sn, base=cs: t.matmul(
                                   p[:, s:s + m],
                                   arnK[c][r:r + 64, k:k + 128],
                                   arnQ[c][r:r + 64, q0 + (s - base):q0 + (s - base) + m],
                                   start=False, stop=True))
                            s0 = s1
                    for j in range(4):
                        emit_score(j)
                    t_sc = mark("tensor", "T")
                    if ki == 0 and pend is not None:
                        st["evacV"] = emit_tail(pend)
                        pend = None
                    st["exp"][par] = op("scalar", lambda s, p=psc, sl=par: s.activation(
                        ptb[:, 768 * sl:768 * sl + 768], p[:, 0:768], AF.Exp,
                        bias=0.0, scale=0.125), waits=(t_sc,), inc="S")
                    t_pv = emit_pv(g, kt, par, Q0, QW, qlo, qhi, k0,
                                   pv_started, st["exp"][par], pv_gate)
                    t_pvz = emit_z(g, kt, par, Q0, QW, qlo, qhi, k0,
                                   z_started, st["evacV"]) or t_pv
                t_cp0 = op("scalar", lambda s, q=QW: s.activation(
                    atmp[:, 0:q], psC[:, 0:q], AF.Copy, bias=0.0, scale=1.0),
                    waits=(t_pvz, st["evacV"]))
                t_cp = op("scalar", lambda s, q=QW: s.activation(
                    atmp[:, 512:512 + q], psC[:, 512:512 + q], AF.Copy,
                    bias=0.0, scale=1.0), inc="S")
                t_c0 = op("vec", lambda v, q=QW: v.tensor_scalar(
                    rztmp[:, 0:q], psA[:, 1024:1024 + q], 1e-30, None, OP.max),
                    waits=(t_pvz,), inc="V")
                op("scalar", lambda s, q=QW: s.activation(
                    rztmp[:, 0:q], rztmp[:, 0:q], AF.Ln,
                    bias=epsz[:, 0:1], scale=1.0), waits=(t_c0,))
                t_e0 = op("scalar", lambda s, q=QW: s.activation(
                    rzsb[0][:, 0:q], rztmp[:, 0:q], AF.Exp,
                    bias=0.0, scale=-1.0), inc="S")
                t_c1 = op("vec", lambda v, q=QW: v.tensor_scalar(
                    xstage[:, 0:q], psB[:, 1024:1024 + q], 1e-30, None, OP.max),
                    inc="V")
                op("scalar", lambda s, q=QW: s.activation(
                    xstage[:, 0:q], xstage[:, 0:q], AF.Ln,
                    bias=epsz[:, 0:1], scale=1.0), waits=(t_c1,))
                t_r1 = op("scalar", lambda s, q=QW: s.activation(
                    rzsb[1][:, 0:q], xstage[:, 0:q], AF.Exp,
                    bias=0.0, scale=-1.0), inc="S")
                pend = {"QW": QW, "g": g, "Q0": Q0, "t_r1": t_r1, "t_cp": t_cp}
            t_evac = emit_tail(pend)
            st["evacV"] = t_evac
            return t_evac

        def emit_pv(g, kt, par, Q0, QW, qlo, qhi, k0, pv_started, t_exp, pv_gate):
            clo = max(qlo, Q0)
            chi = min(qhi, Q0 + QW)
            if clo >= chi:
                return None
            cw = chi - clo
            first = True
            for j in range(4):
                hh = 4 * g + j
                bank = j // 2
                rb = 64 * (j % 2)
                ptoff = 768 * par + 192 * j + (qlo - (k0 - 32)) + (clo - qlo)
                stt = not pv_started[j]
                pv_started[j] = True
                op("tensor", lambda t, k=kt, h_=hh, b=bank, r=rb, po=ptoff, c0=clo,
                   m=cw, st_=stt: t.matmul(
                       psC[r:r + 64, 512 * b + (c0 - Q0):512 * b + (c0 - Q0) + m],
                       arnV[:, 1024 * k + 64 * h_:1024 * k + 64 * h_ + 64],
                       ptb[:, po:po + m], start=st_, stop=True),
                   waits=(t_exp, pv_gate) if first else ())
                first = False
            return mark("tensor", "T")

        def emit_z(g, kt, par, Q0, QW, qlo, qhi, k0, z_started, z_gate):
            clo = max(qlo, Q0)
            chi = min(qhi, Q0 + QW)
            if clo >= chi:
                return None
            cw = chi - clo
            zmap = ((psA, 0), (psA, 32), (psA, 64), (psB, 0))
            zfirst = True
            for j in range(4):
                pz, r = zmap[j]
                ptoff = 768 * par + 192 * j + (qlo - (k0 - 32)) + (clo - qlo)
                stt = not z_started[j]
                z_started[j] = True
                op("tensor", lambda t, p=pz, ro=r, po=ptoff, c0=clo, m=cw,
                   st_=stt: t.matmul(
                       p[ro:ro + 1, 1024 + (c0 - Q0):1024 + (c0 - Q0) + m],
                       onesb[:], ptb[:, po:po + m], start=st_, stop=True),
                   waits=(z_gate,) if (zfirst and stt) else ())
                zfirst = False
            return mark("tensor", "T")

        w_toks = {}
        cross = {"final_V": None, "out_SD": None,
                 "ffnout_V": None, "xbf_next": None}

        def dump(arena, tok):
            for cc in range(NCC):
                op("gdma", lambda g, c=cc: g.dma_start(
                    out=dbg[:, KVA * c:KVA * c + KVA], in_=arena[c][:, 0:KVA]),
                    waits=(tok,), inc="GD")

        def dump1(ap, cols, tok):
            op("gdma", lambda g: g.dma_start(out=dbg[:, 0:cols], in_=ap),
               waits=(tok,), inc="GD")

        def run_halves():
          for h in range(2):
            _CURH[0] = h
            XL = HTOK * h

            PH[0] = f"h{h}:xbf"
            if h == 0:
                t_xbf = None
                xbf_toks = {}
                for bi, (c0, n) in enumerate(BLK_S):
                    nn = KVA - c0 if bi == len(BLK_S) - 1 else n
                    for cc in range(NCC):
                        t_xbf = op("gdma", lambda g, c=cc, a=c0, m=nn, X=XL:
                                   g.dma_start(
                                       out=arnA[c][:, a:a + m],
                                       in_=xTs[128 * c:128 * c + 128,
                                               X + a:X + a + m]), inc="GD")
                        xbf_toks[(cc, bi)] = (t_xbf,)
            else:
                t_xbf = cross["xbf_next"]
                xbf_toks = cross["xbf_toks"]
            if h == 0:
                w_toks["wq"] = load_weight("wq", 0, ())
                w_toks["wk"] = load_weight("wk", 1, ())
            t_x2ld = []
            for cc in range(NCC):
                t_x2ld.append(op("sdma", lambda s, c=cc, X=XL: s.dma_start(
                    out=x2[c][:, 32:1057],
                    in_=xTs[128 * c:128 * c + 128, X + 32:X + 1057]),
                    waits=(cross["out_SD"],) if cc == 0 else (), inc="SD"))

            sA = lambda c, a, m: arnA[c][:, a:a + m]
            sS = lambda c, a, m: arnS[c][:, a:a + m]
            sQ = lambda c, a, m: arnQ[c][:, a:a + m]
            sK = lambda c, a, m: arnK[c][:, a:a + m]
            sX2 = lambda c, a, m: x2[c][:, a:a + m]

            PH[0] = f"h{h}:LN1"
            t_st1 = ln_stats(sA, BLK_S, 0, "1",
                             (t_xbf, t_ones, cross["final_V"], t_consts_gd), sq_fn=sS)
            t_xs1, ap1 = ln_apply2(sA, arnS, arnA, BLK_S, 0, "1", (t_st1,))
            q_toks = {}
            for cc in range(NCC):
                q_toks[(cc, 0)] = (ap1[(0, cc)], ap1[(1, cc)])
                q_toks[(cc, 1)] = (ap1[(1, cc)],)
                q_toks[(cc, 2)] = (ap1[(1, cc)], ap1[(2, cc)])
            k_toks = {(cc, b): (ap1[(b, cc)],) for cc in range(NCC) for b in range(3)}
            if h == DBG_H and DBG == "xs1":
                dump(arnA, t_xs1)
            _ckpt(1)

            PH[0] = f"h{h}:qkv"
            def ev_q(oc, bi, c0, n, pp, w):
                return op("scalar", lambda s, o=oc, a=c0, m=n, b=bi, p=pp: s.activation(
                    arnQ[o][:, a:a + m], p[:, 512 * b:512 * b + m], AF.Identity,
                    bias=qb_s[:, o:o + 1], scale=1.0),
                    waits=tuple(w) + ((cross["final_V"],) if (oc == 0 and bi == 0) else ()),
                    inc="S" if bi == 2 else None)

            t_qmm, t_qe = proj_cm(0, w_toks["wq"], sA, BLK_Q, ev_q, (),
                                  src_toks=q_toks)

            def ev_k(oc, bi, c0, n, pp, w):
                return op("scalar", lambda s, o=oc, a=c0, m=n, b=bi, p=pp: s.activation(
                    arnK[o][:, a:a + m], p[:, 512 * b:512 * b + m], AF.Copy,
                    bias=0.0, scale=1.0),
                    waits=tuple(w) + ((cross["ffnout_V"],) if (oc == 0 and bi == 0) else ()),
                    inc="S" if bi == 2 else None)

            w_toks["wv"] = load_weight("wv", 0, (t_qmm,))
            t_kmm, t_ke = proj_cm(1, w_toks["wk"], sA, BLK_S, ev_k, (),
                                  bias_fold=kbrow, XL=XL, src_toks=k_toks)
            w_toks["wo"] = load_weight("wo", 1, (t_kmm,))
            t_vmm, t_ve = proj_tm(w_toks["wv"], (), XL, src_toks=k_toks)
            w_toks["wr"] = load_weight("wr", 0, (t_vmm,))
            if h == DBG_H and DBG == "q":
                dump(arnQ, t_qe)
            if h == DBG_H and DBG == "k":
                dump(arnK, t_ke)
            if h == DBG_H and DBG == "v":
                for _kt in range(NKT):
                    op("gdma", lambda g, k=_kt: g.dma_start(
                        out=dbg[:, 1024 * k:1024 * k + 1024],
                        in_=arnV[:, 1024 * k:1024 * k + 1024]),
                        waits=(t_ve,), inc="GD")
            _ckpt(2)

            PH[0] = f"h{h}:attn"
            t_attnT = attention((t_qe, t_ke, t_ve))
            if h == DBG_H and DBG == "attnT":
                dump(arnA, t_attnT)
            _ckpt(3)

            PH[0] = f"h{h}:oproj"
            otoks = {}

            def ev_o(oc, bi, c0, n, pp, w):
                tok = op("scalar", lambda s, o=oc, a=c0, m=n, b=bi, p=pp: s.activation(
                    arnK[o][:, a:a + m], p[:, 512 * b:512 * b + m], AF.Identity,
                    bias=ob_s[:, o:o + 1], scale=1.0),
                    waits=w, inc="S" if bi == 2 else None)
                if bi == 2:
                    otoks[oc] = tok
                return tok

            t_omm, t_oe = proj_cm(1, w_toks["wo"], sA, BLK_Q, ev_o, (t_attnT,))
            w_toks["wk2"] = load_weight("wk2", 1, (t_omm,))
            if h == DBG_H and DBG == "attnout":
                dump(arnK, t_oe)
            _ckpt(4)

            PH[0] = f"h{h}:LNa"
            t_sta = ln_stats(sK, BLK_Q, 32, "a", (t_oe,), sq_fn=sS)
            if h == DBG_H and DBG == "rowsa":
                op("gdma", lambda g: g.dma_start(out=dbg[0:1, 0:KVA],
                                                 in_=rMS["a"][0:1, 0:KVA]),
                   waits=(t_sta,), inc="GD")
                op("gdma", lambda g: g.dma_start(out=dbg[1:2, 0:KVA],
                                                 in_=rA["a"][0:1, 0:KVA]),
                   waits=(t_sta,), inc="GD")
            pi = 0
            t_x2 = None
            x2toks = {}
            bctok = [None, None]
            for bi, (c0, n) in enumerate(BLK_Q):
                pb = psC if bi % 2 == 0 else psA
                o0 = c0 - 32
                w0 = [t_sta] + ([bctok[bi % 2]] if bctok[bi % 2] else [])
                t_bca = op("tensor", lambda t, p=pb, o=o0, m=n: t.matmul(
                    p[:, 0:m], ones1[:], rA["a"][0:1, o:o + m], start=True, stop=True),
                    waits=tuple(w0), inc="T")
                for cc in range(NCC):
                    op("vec", lambda v, c=cc, a=c0, m=n, sl=pi % 2, p=pb: v.tensor_tensor(
                        xstage[:, 512 * sl:512 * sl + m], arnK[c][:, a:a + m],
                        p[:, 0:m], OP.mult),
                        waits=(t_bca,) if cc == 0 else ())
                    t_x2 = op("vec", lambda v, c=cc, a=c0, m=n, sl=pi % 2: v.tensor_tensor(
                        x2[c][:, a:a + m], x2[c][:, a:a + m],
                        xstage[:, 512 * sl:512 * sl + m], OP.add),
                        waits=(t_x2ld[-1],), inc="V")
                    x2toks[(bi, cc)] = t_x2
                    pi += 1
                bctok[bi % 2] = t_x2

            if h == DBG_H and DBG == "x2":
                for cc in range(NCC):
                    op("gdma", lambda g, c=cc: g.dma_start(
                        out=dbg[:, KVA * c:KVA * c + KVA], in_=x2[c][:, 0:KVA]),
                        waits=(t_x2,), inc="GD")
            _ckpt(5)
            PH[0] = f"h{h}:LN2"
            cast_toks = {}
            for cc in range(NCC):
                for bi, (c0, n) in enumerate(BLK_Q):
                    cast_toks[(cc, bi)] = (op(
                        "scalar", lambda s, c=cc, a=c0, m=n: s.activation(
                            arnQ[c][:, a:a + m], x2[c][:, a:a + m], AF.Copy,
                            bias=0.0, scale=1.0), waits=(x2toks[(bi, cc)],),
                        inc="S"),)
            t_st2 = ln_stats(sQ, BLK_Q, 32, "2", (t_x2,), sq_fn=sS,
                             src_toks=cast_toks)
            t_a2m = op("vec", lambda v, X=XL: v.tensor_tensor(
                rA["2"][0:1, 0:1025], rA["2"][0:1, 0:1025],
                kvm[0:1, X + 32:X + 32 + 1025], OP.mult), waits=(t_st2,), inc="V")
            t_ln2, _ap2 = ln_apply2(sX2, arnQ, arnQ, BLK_Q, 32, "2", (t_a2m,))

            if h == DBG_H and DBG == "ln2":
                dump(arnQ, t_ln2)
            _ckpt(6)
            PH[0] = f"h{h}:mix"
            t_mix = None
            mix_toks = {}
            for cc in range(NCC):
                for (c0, n) in BLK_O:
                    op("vec", lambda v, c=cc, a=c0, m=n: v.tensor_tensor(
                        arnK[c][:, a:a + m], arnQ[c][:, a:a + m],
                        arnQ[c][:, a - 1:a - 1 + m], OP.subtract),
                        waits=(t_ln2,) if cc == 0 else ())
                for (c0, n) in BLK_O:
                    op("vec", lambda v, c=cc, a=c0, m=n: v.scalar_tensor_tensor(
                        arnV[:, 1152 * c + a:1152 * c + a + m], arnK[c][:, a:a + m],
                        mur_s[:, c:c + 1], arnQ[c][:, a - 1:a - 1 + m],
                        OP.mult, OP.add), waits=())
                for (c0, n) in BLK_O:
                    t_mix = op("vec", lambda v, c=cc, a=c0, m=n: v.scalar_tensor_tensor(
                        arnS[c][:, a:a + m], arnK[c][:, a:a + m],
                        muk_s[:, c:c + 1], arnQ[c][:, a - 1:a - 1 + m],
                        OP.mult, OP.add), waits=(),
                        inc="V" if c0 == 545 else None)
                for b in range(2):
                    mix_toks[(cc, b)] = (t_mix,)

            if h == DBG_H and DBG == "xr":
                dump1(arnV[:, 0:8 * KVA], 8 * KVA, t_mix)
            if h == DBG_H and DBG == "xk":
                dump(arnS, t_mix)
            _ckpt(7)
            PH[0] = f"h{h}:ffn"
            sXR = lambda c, a, m: arnV[:, 1152 * c + a:1152 * c + a + m]

            def ev_r(oc, bi, c0, n, pp, w):
                return op("scalar", lambda s, o=oc, a=c0, m=n, b=bi, p=pp: s.activation(
                    arnK[o][:, a:a + m], p[:, 512 * b:512 * b + m], AF.Sigmoid,
                    bias=0.0, scale=1.0),
                    waits=tuple(w) + ((t_mix,) if (oc == 0 and bi == 0) else ()),
                    inc="S" if bi == len(BLK_O) - 1 else None)

            t_rmm, t_re = proj_cm(0, w_toks["wr"], sXR, BLK_O, ev_r, (),
                                  src_toks=mix_toks)
            w_toks["wv2"] = load_weight("wv2", 0, (t_rmm,))
            if h == 0:
                pre = None
                pre_toks = {}
                for cc in range(NCC):
                    pre = op("gdma", lambda g, c=cc: g.dma_start(
                        out=arnA[c][:, 0:KVA],
                        in_=xTs[128 * c:128 * c + 128, HTOK:HTOK + KVA]),
                        waits=(t_omm,) if cc == 0 else (), inc="GD")
                    for b in range(3):
                        pre_toks[(cc, b)] = (pre,)
                cross["xbf_next"] = pre
                cross["xbf_toks"] = pre_toks

            def ev_kr(oc, bi, c0, n, pp, w):
                return op("scalar", lambda s, o=oc, a=c0, m=n, b=bi, p=pp: s.activation(
                    arnV[:, 1152 * o + a:1152 * o + a + m],
                    p[:, 512 * b:512 * b + m], AF.Relu,
                    bias=0.0, scale=1.0), waits=w,
                    inc="S" if bi == len(BLK_O) - 1 else None)

            t_k2mm, t_k2e = proj_cm(1, w_toks["wk2"], sS, BLK_O, ev_kr, (),
                                    src_toks=mix_toks)
            if h == 0:
                w_toks["wk_next"] = load_weight("wk", 1, (t_k2mm,))
            t_sqk = None
            for cc in range(NCC):
                for (c0, n) in BLK_O:
                    t_sqk = op("vec", lambda v, c=cc, a=c0, m=n: v.tensor_tensor(
                        arnV[:, 1152 * c + a:1152 * c + a + m],
                        arnV[:, 1152 * c + a:1152 * c + a + m],
                        arnV[:, 1152 * c + a:1152 * c + a + m],
                        OP.mult), waits=(t_k2e,),
                        inc="V" if (cc == NCC - 1 and c0 == 545) else None)

            ftoks = {}

            def ev_f(oc, bi, c0, n, pp, w):
                tok = op("vec", lambda v, o=oc, a=c0, m=n, b=bi, p=pp: v.tensor_tensor(
                    arnQ[o][:, a:a + m], p[:, 512 * b:512 * b + m], arnK[o][:, a:a + m],
                    OP.mult), waits=tuple(w) + ((t_re,) if (oc == 0 and bi == 0) else ()),
                    inc="V" if bi == len(BLK_O) - 1 else None)
                if bi == len(BLK_O) - 1:
                    ftoks[oc] = tok
                return tok

            t_v2mm, t_fe = proj_cm(0, w_toks["wv2"], sXR, BLK_O, ev_f, (t_sqk,))
            cross["ffnout_V"] = t_fe
            if h == 0:
                w_toks["wq_next"] = load_weight("wq", 0, (t_v2mm,))

            PH[0] = f"h{h}:LNfout"
            if h == DBG_H and DBG == "r":
                dump(arnK, t_re)
            if h == DBG_H and DBG == "kact":
                dump(arnA, t_sqk)
            if h == DBG_H and DBG == "ffnout":
                dump(arnQ, t_fe)
            _ckpt(8)
            t_stf = ln_stats(sQ, BLK_O, 33, "f", (t_fe,), sq_fn=sS, want_ms=False)
            op("vec", lambda v: v.tensor_tensor(
                rT1[0:1, 0:1024], rMS["a"][0:1, 1:1025], rSQ[0:1, 0:1024], OP.mult),
                waits=(t_stf,))
            t_d = op("vec", lambda v: v.tensor_tensor(
                rD[0:1, 0:1024], rT1[0:1, 0:1024], psA[0:1, 0:1024], OP.add), inc="V")

            if h == DBG_H and DBG == "rows":
                op("gdma", lambda g: g.dma_start(out=dbg[0:1, 0:KVA],
                                                 in_=rA["f"][0:1, 0:KVA]),
                   waits=(t_d,), inc="GD")
                op("gdma", lambda g: g.dma_start(out=dbg[1:2, 0:KVA],
                                                 in_=rD[0:1, 0:KVA]),
                   waits=(t_d,), inc="GD")
                op("gdma", lambda g: g.dma_start(out=dbg[2:3, 0:KVA],
                                                 in_=rMS["a"][0:1, 0:KVA]),
                   waits=(t_d,), inc="GD")
            t_out = None
            bctok2 = [None, None]
            t_fin = None
            for bi, (c0, n) in enumerate(BLK_O):
                pb = psC if bi % 2 == 0 else psA
                o0 = c0 - 33
                w0 = [t_d] + ([bctok2[bi % 2]] if bctok2[bi % 2] else [])
                op("tensor", lambda t, p=pb, o=o0, m=n: t.matmul(
                    p[:, 0:m], onesdiv[:], rD[0:1, o:o + m], start=True, stop=True),
                    waits=tuple(w0))
                t_bcf = op("tensor", lambda t, p=pb, o=o0, m=n: t.matmul(
                    p[:, 512:512 + m], ones1[:], rA["f"][0:1, o:o + m],
                    start=True, stop=True), inc="T")
                for cc in range(NCC):
                    op("vec", lambda v, c=cc, a=c0, m=n, p=pb: v.tensor_tensor(
                        bstage[:, 0:m], arnQ[c][:, a:a + m], p[:, 0:m], OP.subtract),
                        waits=(t_bcf,) if cc == 0 else ())
                    op("vec", lambda v, c=cc, a=c0, m=n, p=pb: v.tensor_tensor(
                        xstage[:, 0:m], bstage[:, 0:m], p[:, 512:512 + m], OP.mult),
                        waits=())
                    t_fin = op("vec", lambda v, c=cc, a=c0, m=n: v.tensor_tensor(
                        x2[c][:, a:a + m], xstage[:, 0:m], x2[c][:, a:a + m], OP.add),
                        inc="V")
                    t_out = op("sdma", lambda s, c=cc, a=c0, m=n, hh=h: s.dma_start(
                        out=yT[128 * c:128 * c + 128,
                               HTOK * hh + a - 33:HTOK * hh + a - 33 + m],
                        in_=x2[c][:, a:a + m]), waits=(t_fin,), inc="SD")
                bctok2[bi % 2] = t_fin
            cross["final_V"] = t_fin
            cross["out_SD"] = t_out
            if h == 0:
                w_toks["wq"] = w_toks.pop("wq_next")
                w_toks["wk"] = w_toks.pop("wk_next")
                _ckpt(9)

        try:
            run_halves()
        except _StopBuild:
            pass

        waited = {}

        def emit_stream(eng_obj, eng_name):
            for (e, builder, ws, inc, amt, _phl) in OPS:
                if e != eng_name:
                    continue
                for (sname, val) in ws:
                    key = (eng_name, sname)
                    if waited.get(key, -1) >= val:
                        continue
                    waited[key] = val
                    eng_obj.wait_ge(sems[sname], val)
                ins = builder(eng_obj)
                if inc is not None:
                    ins.then_inc(sems[inc], amt)

        @block.tensor
        def _(t):
            emit_stream(t, "tensor")

        @block.vector
        def _(v):
            emit_stream(v, "vec")

        @block.scalar
        def _(s):
            emit_stream(s, "scalar")

        @block.gpsimd
        def _(g):
            emit_stream(g, "gdma")
            g.wait_ge(sems["GD"], CNT["GD"])

        @block.sync
        def _(s):
            emit_stream(s, "sdma")
            s.wait_ge(sems["SD"], CNT["SD"])

        _BUILD_LOG["ops"] = [(e, tuple(ws), inc, phl)
                             for (e, b, ws, inc, amt, phl) in OPS]
    return nc


def _host_prep(inputs):
    x = np.asarray(inputs["L0_attn_hidden_states"], dtype=np.float32).reshape(SQ, CM)
    xT = np.ascontiguousarray(x.T)
    XP = np.zeros((CM, 33 + SQ + (XCOLS - 33 - TOK)), dtype=np.float32)
    XP = np.zeros((CM, SQ + XCOLS), dtype=np.float32)
    XP[:, 33:33 + SQ] = xT
    kvm_full = np.zeros((1, SQ + XCOLS), dtype=np.float32)
    kvm_full[0, 33:33 + SQ] = 1.0

    def wt(n):
        return np.ascontiguousarray(np.asarray(inputs[n], dtype=np.float32).T)

    perm = np.concatenate([np.r_[64 * c:64 * c + 64, 64 * (c + 8):64 * (c + 8) + 64]
                           for c in range(8)])
    w = {
        "wq": np.ascontiguousarray(wt("L0_attn_q_w")[:, perm]),
        "wk": np.ascontiguousarray(wt("L0_attn_k_w")[:, perm]),
        "wv": wt("L0_attn_v_w"),
        "wo": wt("L0_attn_o_w"), "wr": wt("L0_ffn_W_r"), "wk2": wt("L0_ffn_W_k"),
        "wv2": wt("L0_ffn_W_v"),
    }

    def col8(n):
        return np.ascontiguousarray(
            np.asarray(inputs[n], dtype=np.float32).reshape(8, 128).T)

    qb8 = np.ascontiguousarray(
        np.asarray(inputs["L0_attn_q_b"], dtype=np.float32)[perm].reshape(8, 128).T)
    ob8 = col8("L0_attn_o_b")
    mur8 = col8("L0_ffn_mu_r")
    muk8 = col8("L0_ffn_mu_k")
    kbrow = np.asarray(inputs["L0_attn_k_b"], dtype=np.float32)[perm].reshape(1, CM)
    vbrow = np.asarray(inputs["L0_attn_v_b"], dtype=np.float32).reshape(1, CM)

    kk = np.arange(128)[:, None]
    qq = np.arange(192)[None, :]
    band = ((qq - kk >= 0) & (qq - kk <= 64))
    m1 = np.where(band, 0.0, MASKVAL).astype(np.float32)
    mask4 = np.ascontiguousarray(np.tile(m1, (1, 4)))
    idn = np.eye(128, dtype=np.float32)

    in_maps = []
    for c in range(8):
        s = TOK * c
        im = {
            "xTs": np.ascontiguousarray(XP[:, s:s + XCOLS]),
            "kvm": np.ascontiguousarray(kvm_full[:, s:s + XCOLS]),
            "qb8": qb8, "ob8": ob8, "mur8": mur8, "muk8": muk8,
            "kbrow": kbrow, "vbrow": vbrow, "mask4": mask4, "idn": idn,
        }
        im.update(w)
        in_maps.append(im)
    return in_maps


def kernel(**inputs):
    in_maps = _host_prep(inputs)
    if "nc" not in _NC_CACHE:
        _NC_CACHE["nc"] = build_program()
    nc = _NC_CACHE["nc"]
    res = run_bass_kernel_spmd(nc, in_maps, core_ids=list(range(8)))
    yT_full = np.concatenate([np.asarray(r["yT"], dtype=np.float32)
                              for r in res.results], axis=1)
    out = np.ascontiguousarray(yT_full.T).reshape(1, SQ, CM).astype(np.float32)
    return out
